# version 29
# baseline (speedup 1.0000x reference)
import numpy as np
import concourse.bass as bass
import concourse.mybir as mybir
from concourse.bass_utils import run_bass_kernel_spmd

F32 = mybir.dt.float32
BF16 = mybir.dt.bfloat16
AF = mybir.ActivationFunctionType
ALU = mybir.AluOpType
AX = mybir.AxisListType

NSEQ = 4
T = 2048
C = 1024
NH = 16
HD = 64
NE = 8
QT = 512
NKC = T // 128
NQT = T // QT
NSAFE = 4


class _Instr:
    __slots__ = ("engine", "fn", "flagged", "pos", "gseq", "sem_val", "sem_key", "dbg")

    def __init__(self, engine, fn, pos, gseq, dbg):
        self.engine = engine
        self.fn = fn
        self.flagged = False
        self.pos = pos
        self.gseq = gseq
        self.sem_val = None
        self.sem_key = None
        self.dbg = dbg


class TBuf:
    __slots__ = ("name", "rng", "last_write", "readers", "overlaps")

    def __init__(self, name, rng=None):
        self.name = name
        self.rng = rng
        self.last_write = None
        self.readers = []
        self.overlaps = []


class Sched:
    ENGINES = ("tensor", "vector", "scalar", "sync", "gpsimd")
    DMA_ENGINES = ("sync", "gpsimd")

    def __init__(self):
        self.q = {e: [] for e in self.ENGINES}
        self.deps = {e: [] for e in self.ENGINES}
        self.bufs = {}
        self.arena_bufs = []

    def buf(self, name, rng=None):
        b = self.bufs.get(name)
        if b is None:
            b = TBuf(name, rng)
            self.bufs[name] = b
            if rng is not None:
                for o in self.arena_bufs:
                    if o.rng[0] < rng[1] and rng[0] < o.rng[1]:
                        o.overlaps.append(b)
                        b.overlaps.append(o)
                self.arena_bufs.append(b)
        return b

    def op(self, engine, fn, reads=(), writes=(), dbg=""):
        self.n_emitted = getattr(self, "n_emitted", 0) + 1
        ins = _Instr(engine, fn, len(self.q[engine]), self.n_emitted, dbg)
        deps = []
        for b in reads:
            if b.last_write is not None:
                deps.append(b.last_write)
        for b in writes:
            grp = [b] + b.overlaps
            for m in grp:
                if m.last_write is not None:
                    deps.append(m.last_write)
                deps.extend(m.readers)
        self.q[engine].append(ins)
        self.deps[engine].append(deps)
        for b in reads:
            b.readers.append(ins)
        for b in writes:
            for m in [b] + b.overlaps:
                m.last_write = ins
                m.readers = []
        return ins

    def finalize(self, nc):
        for e in self.ENGINES:
            for i, deps in enumerate(self.deps[e]):
                for d in deps:
                    if d.engine == e == "tensor":
                        continue
                    if d.engine != e or i - d.pos < NSAFE:
                        d.flagged = True
        counters = {}
        allins = [ins for e in self.ENGINES for ins in self.q[e] if ins.flagged]
        allins.sort(key=lambda i: i.gseq)
        for ins in allins:
            if ins.sem_key is not None:
                k = ins.sem_key
                counters[k] = counters.get(k, 0) + 16
            else:
                k = f"eng_{ins.engine}"
                counters[k] = counters.get(k, 0) + 1
                ins.sem_key = k
            ins.sem_val = counters[k]
        sems = {k: nc.semaphore(f"s_{k}").__enter__() for k in counters}
        for e in self.ENGINES:
            sems.setdefault(f"eng_{e}", None)
        self.sems = sems
        sched = self

        with nc.Block() as block:
            def mk_body(e):
                def body(eng):
                    waited = {}
                    for i, ins in enumerate(sched.q[e]):
                        need = {}
                        for d in sched.deps[e][i]:
                            if d.engine == e == "tensor":
                                continue
                            if d.engine == e and i - d.pos >= NSAFE:
                                continue
                            assert d.sem_val is not None, f"dep unflagged: {d.dbg}"
                            if d.sem_val > need.get(d.sem_key, 0):
                                need[d.sem_key] = d.sem_val
                        for k in sorted(need):
                            if waited.get(k, 0) < need[k]:
                                eng.wait_ge(sems[k], need[k])
                                waited[k] = need[k]
                        ret = ins.fn(eng)
                        if ins.flagged:
                            assert ret is not None, f"no carrier: {e}:{i} {ins.dbg}"
                            inc = 16 if ins.sem_key in counters and not ins.sem_key.startswith("eng_") else 1
                            ret.then_inc(sems[ins.sem_key], inc)
                return body

            for e in self.ENGINES:
                if self.q[e]:
                    getattr(block, e)(mk_body(e))
        return sems


ARENA_BYTES = 190464

O_QB = 0
O_KB = 32768
O_VSB = 65536
O_WDB0 = 98816
O_WDB1 = 107008
O_XB = 115200
O_QEB = 147968
O_C1 = 150016
O_C2 = 152064
O_PT = 115200
O_YSB = 118272
O_X1P = 126464
O_RESST = 142848
O_GU0 = 32768
O_GU1 = 65536
O_WD = 98304
O_MSC0 = 114688
O_MSC1 = 122880
O_SL = 131072
O_DSTG = 135168
O_ACC = 139264
O_CREP = 172032
O_SREP = 176128
O_WB = 180224
O_AB = 182272
O_BB = 184320
O_RECB = 186368
O_X1PB = 188416
O_SQB = 189440
assert O_SQB + 1024 <= ARENA_BYTES


def build_program(nseq=NSEQ, dev_spill_out=False):
    nc = bass.Bass("TRN2", target_bir_lowering=False)
    S = Sched()

    xT = nc.dram_tensor("xT", [nseq, C, T], F32, kind="ExternalInput").ap()
    wqkvp = nc.dram_tensor("wqkvp", [4, C, C], F32, kind="ExternalInput").ap()
    b_arr = nc.dram_tensor("b_arr", [128, 32], F32, kind="ExternalInput").ap()
    bvB_in = nc.dram_tensor("bvB_in", [128, C], F32, kind="ExternalInput").ap()
    gate_w_in = nc.dram_tensor("gate_w_in", [C, NE], F32, kind="ExternalInput").ap()
    sw = nc.dram_tensor("sw", [3, C, C], F32, kind="ExternalInput").ap()
    ew = nc.dram_tensor("ew", [NE, 3, C, C], F32, kind="ExternalInput").ap()
    crep_in = nc.dram_tensor("crep_in", [128, T], F32, kind="ExternalInput").ap()
    srep_in = nc.dram_tensor("srep_in", [128, T], F32, kind="ExternalInput").ap()
    sswap_in = nc.dram_tensor("sswap_in", [128, 128], F32, kind="ExternalInput").ap()
    tri_in = nc.dram_tensor("tri_in", [128, 128], F32, kind="ExternalInput").ap()
    ident_in = nc.dram_tensor("ident_in", [128, 128], F32, kind="ExternalInput").ap()
    basis_in = nc.dram_tensor("basis_in", [NE, NE * 128], F32, kind="ExternalInput").ap()
    outT = nc.dram_tensor("outT", [nseq, C, T], F32, kind="ExternalOutput").ap()
    spill_kind = "ExternalOutput" if dev_spill_out else "Internal"
    spill = nc.dram_tensor("spill", [nseq, C, T], F32, kind=spill_kind).ap()
    ewbf = nc.dram_tensor("ewbf", [NE, 3, C, C], BF16, kind="Internal").ap()
    swbf = nc.dram_tensor("swbf", [3, C, C], BF16, kind="Internal").ap()
    if dev_spill_out:
        wdbg = nc.dram_tensor("wdbg", [nseq, NE, T], BF16, kind="ExternalOutput").ap()
        ydbg = nc.dram_tensor("ydbg", [128, 8 * QT], BF16, kind="ExternalOutput").ap()
        xpdbg = nc.dram_tensor("xpdbg", [128, 8 * QT], F32, kind="ExternalOutput").ap()

    arena = nc.alloc_sbuf_tensor("arena", [128, ARENA_BYTES], mybir.dt.uint8)

    def av(off, nbytes, dt):
        return arena[:, off:off + nbytes].bitcast(dt)

    b_sb = nc.alloc_sbuf_tensor("b_sb", [128, 32], F32)
    bvB = nc.alloc_sbuf_tensor("bvB", [128, C], BF16)
    sswap = nc.alloc_sbuf_tensor("sswap", [128, 128], BF16)
    tri = nc.alloc_sbuf_tensor("tri", [128, 128], BF16)
    identf = nc.alloc_sbuf_tensor("identf", [128, 128], F32)
    ones_col_b = nc.alloc_sbuf_tensor("ones_col_b", [128, 1], BF16)
    onesF = nc.alloc_sbuf_tensor("onesF", [128, 128], F32)
    selB = nc.alloc_sbuf_tensor("selB", [128, 128], F32)
    rec2b = nc.alloc_sbuf_tensor("rec2b", [128, QT], F32)
    gw_sb = nc.alloc_sbuf_tensor("gw_sb", [128, 8 * NE], BF16)
    basis_sb = nc.alloc_sbuf_tensor("basis_sb", [NE, NE * 128], BF16)
    wT_sb = nc.alloc_sbuf_tensor("wT_sb", [NE, T], BF16)
    rows4 = nc.alloc_sbuf_tensor("rows4", [128, 3 * QT], F32)
    gsc = nc.alloc_sbuf_tensor("gsc", [128, 448], F32)

    r0 = rows4[0:1, 0:QT]
    r1 = rows4[0:1, QT:2 * QT]
    r2 = rows4[0:1, 2 * QT:3 * QT]
    rset = [(r0, r1, r2),
            (rows4[32:33, 0:QT], rows4[32:33, QT:2 * QT], rows4[32:33, 2 * QT:3 * QT])]
    rec_rows = [[rec2b[0:1, :], rec2b[32:33, :]],
                [rec2b[64:65, :], rec2b[96:97, :]]]
    rec2v = [rec2b[0:33, :], rec2b[64:97, :]]
    QB = av(O_QB, 32768, BF16).rearrange("p (k t) -> p k t", t=T)
    KB = av(O_KB, 32768, BF16).rearrange("p (k t) -> p k t", t=T)
    V4 = av(O_VSB, 33280, BF16).rearrange("p (a b c) -> p a b c", b=16, c=65)
    Wdb = [av(O_WDB0, 8192, BF16).rearrange("p (k t) -> p k t", t=512),
           av(O_WDB1, 8192, BF16).rearrange("p (k t) -> p k t", t=512)]
    xb = av(O_XB, 32768, BF16).rearrange("p (k t) -> p k t", t=T)
    qeb = [av(O_QEB + i * 1024, 1024, BF16) for i in range(2)]
    c1 = av(O_C1, 2048, F32)
    c2 = av(O_C2, 2048, F32)
    Pt = [av(O_PT + i * 1024, 1024, BF16) for i in range(3)] + [av(O_QEB, 1024, BF16)]
    y_v = av(O_YSB, 8192, BF16).rearrange("p (k t) -> p k t", t=QT)
    x1pv = av(O_X1P, 16384, F32).rearrange("p (k t) -> p k t", t=QT)
    resst = [av(O_RESST + i * 2048, 2048, F32) for i in range(2)]
    aB = av(O_AB, 2048, F32)
    bB = av(O_BB, 2048, F32)
    recB = av(O_RECB, 2048, F32)
    x1pb = av(O_X1PB, 1024, BF16)
    sqb = av(O_SQB, 1024, BF16)
    crepB = av(O_CREP, 4096, BF16)
    srepB = av(O_SREP, 4096, BF16)
    x1nbv = QB
    guv = [av(O_GU0 + j * 32768, 32768, BF16).rearrange("p (m k t) -> p m k t",
                                                        m=2, t=1024)
           for j in range(2)]
    wdv = av(O_WD, 16384, BF16).rearrange("p (k t) -> p k t", t=1024)
    mscv = [av(O_MSC0 + j * 8192, 8192, BF16).rearrange("p (k t) -> p k t", t=QT)
            for j in range(2)]
    accv = av(O_ACC, 32768, F32).rearrange("p (k t) -> p k t", t=1024)
    sl = [av(O_SL + i * 2048, 2048, F32) for i in range(2)]
    dstg = [av(O_DSTG + i * 2048, 2048, F32) for i in range(2)]
    wBsb = [av(O_WB + i * 1024, 1024, BF16) for i in range(2)]
    gA, gB, gC = gsc[:, 0:128], gsc[:, 128:256], gsc[:, 256:384]
    gA3 = gA.rearrange("p (g e) -> p g e", e=8)
    gB3 = gB.rearrange("p (g e) -> p g e", e=8)
    gC3 = gC.rearrange("p (g e) -> p g e", e=8)
    gm1, gm2, grd = gsc[:, 384:400], gsc[:, 400:416], gsc[:, 416:432]
    gm13 = gm1.rearrange("p (g o) -> p g o", o=1)
    gm23 = gm2.rearrange("p (g o) -> p g o", o=1)
    grd3 = grd.rearrange("p (g o) -> p g o", o=1)

    PB = [nc.alloc_psum_tensor(f"pb{i}", [128, 512], F32) for i in range(8)]

    def tb(name):
        return S.buf(name)

    def ab_(name, off, nbytes):
        return S.buf(name, (off, off + nbytes))

    t_QBt = [tb(f"QBt{j}") for j in range(4)]
    t_KB = ab_("KB", O_KB, 32768)
    t_V = ab_("Vsb", O_VSB, 33280)
    t_Wdb = [ab_("Wdb0", O_WDB0, 8192), ab_("Wdb1", O_WDB1, 8192)]
    t_xbc = [tb(f"xbc{j}") for j in range(4)]
    t_xb_rng = ab_("xbrng", O_XB, 32768)
    t_qeb = [ab_(f"qeb{i}", O_QEB + i * 1024, 1024) for i in range(2)]
    t_c1 = ab_("c1", O_C1, 2048)
    t_c2 = ab_("c2", O_C2, 2048)
    t_Pt = [ab_(f"Pt{i}", O_PT + i * 1024, 1024) for i in range(3)] + [ab_("Pt3", O_QEB, 1024)]
    t_ysb = ab_("ysb", O_YSB, 8192)
    t_x1p = ab_("x1p", O_X1P, 16384)
    t_resst = [ab_(f"resst{i}", O_RESST + i * 2048, 2048) for i in range(2)]
    t_gu = [ab_(f"gu{j}", O_GU0 + j * 32768, 32768) for j in range(2)]
    t_wd = ab_("wd", O_WD, 16384)
    t_msc = [ab_(f"msc{j}", O_MSC0 + j * 8192, 8192) for j in range(2)]
    t_acc = ab_("acc", O_ACC, 32768)
    t_sl = [ab_(f"sl{i}", O_SL + i * 2048, 2048) for i in range(2)]
    t_dstg = [ab_(f"dstg{i}", O_DSTG + i * 2048, 2048) for i in range(2)]
    t_crep = ab_("crepB", O_CREP, 4096)
    t_srep = ab_("srepB", O_SREP, 4096)
    t_x1pb = ab_("x1pb", O_X1PB, 1024)
    t_sqb = ab_("sqb", O_SQB, 1024)
    t_aB = ab_("aB", O_AB, 2048)
    t_bB = ab_("bB", O_BB, 2048)
    t_recB = ab_("recB", O_RECB, 2048)
    t_wB = [ab_(f"wB{i}", O_WB + i * 1024, 1024) for i in range(2)]
    t_gsc = tb("gsc")
    t_pb = [tb(f"pb{i}") for i in range(8)]
    t_pb5s = [tb(f"pb5s{k}") for k in range(4)]
    for k in range(4):
        t_pb5s[k].overlaps.append(t_pb[5])
        t_pb[5].overlaps.append(t_pb5s[k])

    def dma(engine, out_ap, in_ap, reads=(), writes=(), dbg="", skey=None):
        ins = S.op(engine, lambda e, o=out_ap, i=in_ap: e.dma_start(out=o, in_=i),
                   reads=reads, writes=writes, dbg=dbg)
        ins.flagged = True
        assert writes, f"dma without written buffer: {dbg}"
        ins.sem_key = skey if skey is not None else ("d_" + writes[0].name)
        return ins

    dma("sync", b_sb[:], b_arr, writes=[tb("b_sb")], dbg="ld b_sb")
    dma("gpsimd", bvB[:], bvB_in, writes=[tb("bvB")], dbg="ld bvB")
    dma("gpsimd", tri[:], tri_in, writes=[tb("tri")], dbg="ld tri")
    dma("sync", identf[:], ident_in, writes=[tb("identf")], dbg="ld identf")
    dma("gpsimd", sswap[:], sswap_in, writes=[tb("sswap")], dbg="ld sswap")
    dma("gpsimd", basis_sb[:], basis_in, writes=[tb("basis")], dbg="ld basis")
    dma("gpsimd", gw_sb[:].rearrange("p (c e) -> p c e", c=8),
        gate_w_in.rearrange("(c p) e -> p c e", p=128), writes=[tb("gw_sb")], dbg="ld gw")
    S.op("vector", lambda e: e.memset(ones_col_b[:], 1.0), writes=[tb("ones_col_b")],
         dbg="ones col")
    S.op("vector", lambda e: e.memset(onesF[:], 1.0), writes=[tb("onesF")],
         dbg="ones f")
    S.op("vector", lambda e: e.memset(selB[:], 0.0), writes=[tb("selB")], dbg="selB 0")
    S.op("vector", lambda e: e.memset(selB[0:1, 0:64], 1.0), writes=[tb("selB")], dbg="selB a")
    S.op("vector", lambda e: e.memset(selB[32:33, 64:128], 1.0), writes=[tb("selB")], dbg="selB b")
    S.op("vector", lambda e: e.memset(selB[64:65, 0:64], 1.0), writes=[tb("selB")], dbg="selB c")
    S.op("vector", lambda e: e.memset(selB[96:97, 64:128], 1.0), writes=[tb("selB")], dbg="selB d")
    S.op("vector", lambda e: e.memset(rec2b[:], 1.0),
         writes=[tb("rec2"), tb("rec2_0"), tb("rec2_1")], dbg="rec2 1")
    dma("gpsimd", crepB, crep_in, writes=[t_crep], dbg="ld crep")
    dma("gpsimd", srepB, srep_in, writes=[t_srep], dbg="ld srep")

    for s in range(nseq):
        t_spill4 = [tb(f"spill_{s}_{i}") for i in range(4)]
        S.op("vector", lambda e: e.memset(V4[:, :, :, 64:65], 1.0), writes=[t_V],
             dbg=f"s{s} V ones")
        xsrc = xT[s].rearrange("(c p) t -> p c t", p=128)
        for j in range(4):
            dma("gpsimd", xb[:, :, j * QT:(j + 1) * QT], xsrc[:, :, j * QT:(j + 1) * QT],
                writes=[t_xbc[j], t_xb_rng], dbg=f"s{s} ld xb{j}")

        def emit_w_load(wi):
            w, h = wi // 2, wi % 2
            bufi = wi % 2
            dma("gpsimd", Wdb[bufi],
                wqkvp[w].rearrange("(c p) o -> p c o", p=128)[:, :, h * 512:(h + 1) * 512],
                writes=[t_Wdb[bufi]], dbg=f"s{s} ldW{w}h{h}")

        emit_w_load(0)
        emit_w_load(1)
        qk_i = 0
        for wi in range(8):
            w, h = wi // 2, wi % 2
            bufi = wi % 2
            wv = Wdb[bufi]
            if w in (0, 1):
                dstv = QB if w == 0 else KB
                for qt in range(NQT):
                    ts_ = slice(qt * QT, (qt + 1) * QT)
                    t_dst = t_QBt[qt] if w == 0 else t_KB
                    for oc in range(h * 4, h * 4 + 4):
                        bk = (0, 1, 3, 4)[qk_i % 4]
                        sbk2 = (2, 5)[qk_i % 2]
                        qbi = qk_i % 2
                        qk_i += 1
                        lo = (oc - h * 4) * 128

                        def mm_qkv(e, wv=wv, lo=lo, ts_=ts_, bk=bk):
                            last = None
                            for cc in range(8):
                                last = e.matmul(PB[bk][:], wv[:, cc, lo:lo + 128],
                                                xb[:, cc, ts_], start=(cc == 0), stop=(cc == 7))
                            return last
                        S.op("tensor", mm_qkv,
                             reads=[t_Wdb[bufi], t_xbc[qt], t_xb_rng], writes=[t_pb[bk]],
                             dbg=f"s{s}w{w}oc{oc}q{qt} mm")
                        bia = b_sb[:, w * 8 + oc:w * 8 + oc + 1]
                        S.op("scalar", lambda e, bk=bk, bia=bia, qbi=qbi: e.activation(
                            qeb[qbi][:], PB[bk][:], AF.Identity, bias=bia),
                            reads=[t_pb[bk], tb("b_sb")], writes=[t_qeb[qbi]],
                            dbg=f"s{s}w{w}oc{oc}q{qt} qeb")
                        S.op("tensor", lambda e, qbi=qbi, sbk2=sbk2: e.matmul(
                            PB[sbk2][:], sswap[:], qeb[qbi][:], start=True, stop=True),
                            reads=[tb("sswap"), t_qeb[qbi]], writes=[t_pb[sbk2]],
                            dbg=f"s{s}w{w}oc{oc}q{qt} swap")
                        S.op("vector", lambda e, ts_=ts_, sbk2=sbk2: e.tensor_tensor(
                            c2[:], PB[sbk2][:], srepB[:, ts_], ALU.mult),
                            reads=[t_pb[sbk2], t_srep], writes=[t_c2], dbg="rope c2")
                        S.op("vector", lambda e, ts_=ts_, qbi=qbi: e.tensor_tensor(
                            c1[:], qeb[qbi][:], crepB[:, ts_], ALU.mult),
                            reads=[t_qeb[qbi], t_crep], writes=[t_c1], dbg="rope c1")
                        S.op("vector", lambda e, dstv=dstv, oc=oc, ts_=ts_: e.tensor_tensor(
                            dstv[:, oc, ts_], c1[:], c2[:], ALU.add),
                            reads=[t_c1, t_c2], writes=[t_dst], dbg="rope sum")
            elif w == 2:
                for tt in range(NKC):
                    bk = (0, 1, 3, 4)[tt % 4]

                    def mm_v(e, wv=wv, tt=tt, bk=bk):
                        last = None
                        for cc in range(8):
                            last = e.matmul(PB[bk][:], xb[:, cc, tt * 128:(tt + 1) * 128],
                                            wv[:, cc, :], start=(cc == 0), stop=(cc == 7))
                        return last
                    S.op("tensor", mm_v, reads=[t_Wdb[bufi], t_xbc[tt // 4], t_xb_rng],
                         writes=[t_pb[bk]], dbg=f"s{s}V t{tt}h{h}")
                    pin = PB[bk][:].rearrange("p (h d) -> p h d", h=8)[:, :, 0:64]
                    vout = V4[:, tt, h * 8:(h + 1) * 8, 0:64]
                    bin_ = bvB[:].rearrange("p (h d) -> p h d", h=16)[:, h * 8:(h + 1) * 8, :]
                    S.op("vector", lambda e, vout=vout, pin=pin, bin_=bin_: e.tensor_tensor(
                        vout, pin, bin_, ALU.add),
                        reads=[t_pb[bk], tb("bvB")], writes=[t_V], dbg=f"s{s}V ev t{tt}h{h}")
            if wi + 2 < 8:
                emit_w_load(wi + 2)

        pending_ln = [None]
        for qt in range(NQT):
            ts_ = slice(qt * QT, (qt + 1) * QT)

            def emit_score_pair(hp, kc, qt=qt):
                oc = hp
                j = kc - 4 * qt
                colsoff = max(0, j * 128)
                for par in range(2):
                    rb = par * 64
                    sbk = (kc % 2) * 2 + par
                    S.op("tensor", lambda e, sbk=sbk, rb=rb, oc=oc, kc=kc, colsoff=colsoff, qt=qt:
                         e.matmul(PB[sbk][:, colsoff:QT],
                                  KB[rb:rb + 64, oc, kc * 128:(kc + 1) * 128],
                                  QB[rb:rb + 64, oc, qt * QT + colsoff:(qt + 1) * QT],
                                  start=True, stop=True, tile_position=(rb, 0)),
                         reads=[t_KB, t_QBt[qt]], writes=[t_pb[sbk]],
                         dbg=f"s{s}q{qt}p{hp}.{par}k{kc} sc")

            def emit_py_pair(hp, kc, qt=qt, ts_=ts_):
                kcN = 4 * qt + 4
                j = kc - 4 * qt
                colsoff = max(0, j * 128)
                N = QT - colsoff
                for par in range(2):
                    hh = 2 * hp + par
                    sbk = (kc % 2) * 2 + par
                    pbi = (kc * 2 + par) % 4
                    S.op("scalar", lambda e, sbk=sbk, pbi=pbi, colsoff=colsoff, N=N:
                         e.activation(Pt[pbi][:, 0:N], PB[sbk][:, colsoff:QT],
                                      AF.Exp, scale=0.125),
                         reads=[t_pb[sbk]], writes=[t_Pt[pbi]],
                         dbg=f"s{s}q{qt}h{hh}k{kc} exp")
                    if j >= 0:
                        S.op("vector", lambda e, pbi=pbi: e.tensor_tensor(
                            Pt[pbi][:, 0:128], Pt[pbi][:, 0:128], tri[:], ALU.mult),
                            reads=[t_Pt[pbi], tb("tri")], writes=[t_Pt[pbi]],
                            dbg=f"s{s}q{qt}h{hh}k{kc} tri")
                for par in range(2):
                    hh = 2 * hp + par
                    pbi = (kc * 2 + par) % 4
                    ybk = 4 + (hp % 2) * 2 + par
                    S.op("tensor", lambda e, ybk=ybk, kc=kc, hh=hh, pbi=pbi, colsoff=colsoff,
                         N=N, kcN=kcN: e.matmul(
                             PB[ybk][0:65, colsoff:QT], V4[:, kc, hh, :], Pt[pbi][:, 0:N],
                             start=(kc == 0), stop=(kc == kcN - 1)),
                         reads=[t_V, t_Pt[pbi]], writes=[t_pb[ybk]],
                         dbg=f"s{s}q{qt}h{hh}k{kc} y")

            def emit_flash(hp, qt=qt):
                kcN = 4 * qt + 4
                emit_score_pair(hp, 0)
                for kc in range(kcN):
                    if kc + 1 < kcN:
                        emit_score_pair(hp, kc + 1)
                    emit_py_pair(hp, kc)
                pp = hp % 2
                for par in range(2):
                    ybk = 4 + pp * 2 + par
                    S.op("scalar", lambda e, ybk=ybk, pp=pp, par=par: e.copy(
                        rec_rows[pp][par], PB[ybk][64:65, :]),
                        reads=[t_pb[ybk]], writes=[tb(f"rec2_{pp}")],
                        dbg=f"s{s}q{qt}p{hp}.{par} rec")
                S.op("vector", lambda e, pp=pp: e.reciprocal(rec2v[pp], rec2v[pp]),
                     reads=[tb(f"rec2_{pp}")], writes=[tb(f"rec2_{pp}")],
                     dbg=f"s{s}q{qt}p{hp} recv")

            def emit_evac(hp, qt=qt):
                oc = hp
                pp = hp % 2
                bc = 0 if pp == 0 else 2
                S.op("tensor", lambda e, pp=pp, bc=bc: e.matmul(
                    PB[bc][:], selB[pp * 64:pp * 64 + 33, :], rec2v[pp],
                    start=True, stop=True),
                    reads=[tb("selB"), tb(f"rec2_{pp}")], writes=[t_pb[bc]],
                    dbg=f"s{s}q{qt}p{hp} bc")
                S.op("scalar", lambda e, bc=bc: e.copy(recB[0:64, :], PB[bc][0:64, :]),
                     reads=[t_pb[bc]], writes=[t_recB], dbg=f"s{s}q{qt}p{hp} bcev")
                S.op("scalar", lambda e, bc=bc: e.copy(c1[0:64, :], PB[bc][64:128, :]),
                     reads=[t_pb[bc]], writes=[t_c1], dbg=f"s{s}q{qt}p{hp} bcev2")
                for par in range(2):
                    ybk = 4 + pp * 2 + par
                    rsrc = recB if par == 0 else c1
                    t_rs = t_recB if par == 0 else t_c1
                    S.op("vector", lambda e, ybk=ybk, par=par, oc=oc, rsrc=rsrc: e.tensor_tensor(
                        y_v[par * 64:par * 64 + 64, oc, :], PB[ybk][0:64, :],
                        rsrc[0:64, :], ALU.mult),
                        reads=[t_pb[ybk], t_rs], writes=[t_ysb],
                        dbg=f"s{s}q{qt}p{hp}.{par} yev")

            def emit_resld(oc, qt=qt, ts_=ts_):
                dma("gpsimd", resst[oc % 2], xT[s][oc * 128:(oc + 1) * 128, ts_],
                    writes=[t_resst[oc % 2]], dbg=f"s{s}q{qt}oc{oc} resld")

            for hp in range(8):
                emit_flash(hp)
                if hp == 1 and pending_ln[0] is not None:
                    pending_ln[0]()
                    pending_ln[0] = None
                if hp == 5:
                    emit_resld(0)
                    emit_resld(1)
                if hp > 0:
                    emit_evac(hp - 1)
            emit_evac(7)
            if dev_spill_out and s == 0 and qt == 0:
                dma("sync", ydbg, av(O_YSB, 8192, BF16), reads=[t_ysb],
                    writes=[tb("ydbg")], dbg="ydbg")
            for oc in range(8):
                sti = oc % 2
                pbk = 5 if oc % 2 == 0 else 2
                half, lo = (0, oc * 128) if oc < 4 else (1, (oc - 4) * 128)

                def mm_proj(e, pbk=pbk, half=half, lo=lo):
                    last = None
                    for cc in range(8):
                        last = e.matmul(PB[pbk][:], Wdb[half][:, cc, lo:lo + 128],
                                        y_v[:, cc, :], start=(cc == 0), stop=(cc == 7))
                    return last
                S.op("tensor", mm_proj, reads=[t_Wdb[half], t_ysb], writes=[t_pb[pbk]],
                     dbg=f"s{s}q{qt}oc{oc} proj")
                bia = b_sb[:, 24 + oc:24 + oc + 1]
                S.op("vector", lambda e, pbk=pbk, oc=oc, bia=bia, sti=sti:
                     e.scalar_tensor_tensor(x1pv[:, oc, :], PB[pbk][:], bia,
                                            resst[sti], ALU.add, ALU.add),
                     reads=[t_pb[pbk], tb("b_sb"), t_resst[sti]], writes=[t_x1p],
                     dbg=f"s{s}q{qt}oc{oc} x1p")
                if oc + 2 < 8:
                    emit_resld(oc + 2)
                S.op("vector", lambda e, oc=oc: e.tensor_copy(x1pb, x1pv[:, oc, :]),
                     reads=[t_x1p], writes=[t_x1pb], dbg="st cast")
                S.op("vector", lambda e, oc=oc: e.tensor_tensor(
                    sqb, x1pv[:, oc, :], x1pv[:, oc, :], ALU.mult),
                    reads=[t_x1p], writes=[t_sqb], dbg="st sq")
                S.op("tensor", lambda e, oc=oc: e.matmul(
                    PB[6][0:1, :], ones_col_b[:], x1pb, start=(oc == 0), stop=(oc == 7)),
                    reads=[tb("ones_col_b"), t_x1pb], writes=[t_pb[6]], dbg="st sum")
                S.op("tensor", lambda e, oc=oc: e.matmul(
                    PB[7][0:1, :], ones_col_b[:], sqb, start=(oc == 0), stop=(oc == 7)),
                    reads=[tb("ones_col_b"), t_sqb], writes=[t_pb[7]], dbg="st sq mm")

            S.op("vector", lambda e: e.tensor_scalar(r0, PB[6][0:1, :], 1.0 / C,
                                                     None, ALU.mult),
                 reads=[t_pb[6]], writes=[tb("r0")], dbg="ln mean")
            S.op("vector", lambda e: e.tensor_scalar(r1, PB[7][0:1, :], 1.0 / C,
                                                     None, ALU.mult),
                 reads=[t_pb[7]], writes=[tb("r1")], dbg="ln ex2")

            def make_ln1(qt=qt, ts_=ts_):
                def emit_ln1():
                    S.op("vector", lambda e: e.tensor_tensor(r2, r0, r0, ALU.mult),
                         reads=[tb("r0")], writes=[tb("r2")], dbg="ln m2")
                    S.op("vector", lambda e: e.tensor_tensor(r1, r1, r2, ALU.subtract),
                         reads=[tb("r1"), tb("r2")], writes=[tb("r1")], dbg="ln var")
                    S.op("vector", lambda e: e.tensor_scalar(r1, r1, 1e-5, None, ALU.add),
                         reads=[tb("r1")], writes=[tb("r1")], dbg="ln eps")
                    S.op("scalar", lambda e: e.activation(r1, r1, AF.Ln),
                         reads=[tb("r1")], writes=[tb("r1")], dbg="ln ln")
                    S.op("scalar", lambda e: e.activation(r2, r1, AF.Exp, scale=-0.5),
                         reads=[tb("r1")], writes=[tb("r2")], dbg="ln rstd")
                    S.op("vector", lambda e: e.tensor_tensor(r0, r0, r2, ALU.mult),
                         reads=[tb("r0"), tb("r2")], writes=[tb("r0")], dbg="ln b0")
                    S.op("vector", lambda e: e.tensor_scalar(r0, r0, -1.0, None, ALU.mult),
                         reads=[tb("r0")], writes=[tb("r0")], dbg="ln b")
                    S.op("tensor", lambda e: e.matmul(PB[1][:], onesF[0:1, :], r2,
                                                      start=True, stop=True),
                         reads=[tb("onesF"), tb("r2")], writes=[t_pb[1]], dbg="ln bca")
                    S.op("scalar", lambda e: e.copy(aB, PB[1][:]),
                         reads=[t_pb[1]], writes=[t_aB], dbg="ln bca ev")
                    S.op("tensor", lambda e: e.matmul(PB[1][:], onesF[0:1, :], r0,
                                                      start=True, stop=True),
                         reads=[tb("onesF"), tb("r0")], writes=[t_pb[1]], dbg="ln bcb")
                    S.op("scalar", lambda e: e.copy(bB, PB[1][:]),
                         reads=[t_pb[1]], writes=[t_bB], dbg="ln bcb ev")


                    for oc in range(8):
                        sti = oc % 2
                        S.op("vector", lambda e, oc=oc: e.tensor_tensor(
                            x1pv[:, oc, :], x1pv[:, oc, :], aB, ALU.mult),
                            reads=[t_x1p, t_aB], writes=[t_x1p], dbg="ln ap1")
                        S.op("vector", lambda e, oc=oc, sti=sti: e.tensor_tensor(
                            resst[sti], x1pv[:, oc, :], bB, ALU.add),
                            reads=[t_x1p, t_bB], writes=[t_resst[sti]], dbg="ln ap2")
                        S.op("vector", lambda e, oc=oc, ts_=ts_, sti=sti: e.tensor_copy(
                            x1nbv[:, oc, ts_], resst[sti]),
                            reads=[t_resst[sti]], writes=[t_QBt[qt]], dbg="ln x1nb")
                        dma("sync", spill[s][oc * 128:(oc + 1) * 128, ts_], resst[sti],
                            reads=[t_resst[sti]], writes=[t_spill4[oc % 4]],
                            dbg=f"s{s}q{qt}oc{oc} spill", skey=f"d_sp{oc % 4}")

                return emit_ln1

            if dev_spill_out and s == 0 and qt == 0:
                dma("sync", xpdbg, av(O_X1P, 16384, F32), reads=[t_x1p],
                    writes=[tb("xpdbg")], dbg="xpdbg")
            pending_ln[0] = make_ln1()
        pending_ln[0]()
        pending_ln[0] = None

        gv = gw_sb[:].rearrange("p (c e) -> p c e", c=8)
        t_wT = tb(f"wT_{s}")

        def emit_glogit(pi):
            def mm_gl(e, pi=pi):
                last = None
                for cc in range(8):
                    last = e.matmul(PB[5][:, 8 * pi:8 * pi + 8],
                                    x1nbv[:, cc, pi * 128:(pi + 1) * 128],
                                    gv[:, cc, :], start=(cc == 0), stop=(cc == 7))
                return last
            S.op("tensor", mm_gl, reads=[tb("gw_sb"), t_QBt[pi // 4]],
                 writes=[t_pb[5]], dbg=f"s{s}g{pi} mm")

        def emit_gchain():
            S.op("scalar", lambda e: e.copy(gA, PB[5][:, 0:128]),
                 reads=[t_pb[5]], writes=[t_gsc], dbg=f"s{s} g evac")
            S.op("vector", lambda e: e.reduce_max(gm13, gA3, AX.X),
                 reads=[t_gsc], writes=[t_gsc], dbg="g m1")
            S.op("vector", lambda e: e.tensor_tensor(
                gB3, gA3, gm13.broadcast_to([128, 16, 8]), ALU.subtract),
                reads=[t_gsc], writes=[t_gsc], dbg="g sub")
            S.op("scalar", lambda e: e.activation(gA, gB, AF.Exp),
                 reads=[t_gsc], writes=[t_gsc], dbg="g exp")
            S.op("vector", lambda e: e.reduce_max(gm13, gA3, AX.X),
                 reads=[t_gsc], writes=[t_gsc], dbg="g m1b")
            S.op("vector", lambda e: e.tensor_tensor(
                gB3, gA3, gm13.broadcast_to([128, 16, 8]), ALU.is_ge),
                reads=[t_gsc], writes=[t_gsc], dbg="g geq")
            S.op("vector", lambda e: e.scalar_tensor_tensor(
                gC, gB, -1e30, gA, ALU.mult, ALU.add),
                reads=[t_gsc], writes=[t_gsc], dbg="g gg2")
            S.op("vector", lambda e: e.reduce_max(gm23, gC3, AX.X),
                 reads=[t_gsc], writes=[t_gsc], dbg="g m2")
            S.op("vector", lambda e: e.tensor_tensor(
                gB3, gA3, gm23.broadcast_to([128, 16, 8]), ALU.is_ge),
                reads=[t_gsc], writes=[t_gsc], dbg="g sel")
            S.op("vector", lambda e: e.tensor_tensor(gC, gA, gB, ALU.mult),
                 reads=[t_gsc], writes=[t_gsc], dbg="g num")
            S.op("vector", lambda e: e.tensor_tensor(gm1, gm1, gm2, ALU.add),
                 reads=[t_gsc], writes=[t_gsc], dbg="g den")
            S.op("vector", lambda e: e.reciprocal(grd, gm1),
                 reads=[t_gsc], writes=[t_gsc], dbg="g rd")
            S.op("vector", lambda e: e.tensor_tensor(
                gC3, gC3, grd3.broadcast_to([128, 16, 8]), ALU.mult),
                reads=[t_gsc], writes=[t_gsc], dbg="g w")

        for pi in range(12):
            emit_glogit(pi)

        def emit_gtr(pi):
            cs = (pi % 4) * 128
            S.op("tensor", lambda e, pi=pi, cs=cs: e.transpose(
                PB[5][0:8, cs:cs + 128], gC3[:, pi, :], identf[:]),
                reads=[t_gsc, tb("identf")], writes=[t_pb5s[pi % 4]],
                dbg=f"s{s} gtr{pi}")
            S.op("scalar", lambda e, pi=pi, cs=cs: e.copy(
                wT_sb[0:8, pi * 128:(pi + 1) * 128], PB[5][0:8, cs:cs + 128]),
                reads=[t_pb5s[pi % 4]], writes=[t_wT], dbg=f"s{s} gtrc{pi}")

        first_pass_s = (s == 0)

        def emit_gu_load(gi):
            buf = gi % 2
            q = "gpsimd"
            e_id = (gi % 9) - 1
            if first_pass_s and gi <= 8:
                srcs = [sw[j] for j in range(2)] if e_id < 0 else [ew[e_id, j] for j in range(2)]
                dsts = [swbf[j] for j in range(2)] if e_id < 0 else [ewbf[e_id, j] for j in range(2)]
                for j in range(2):
                    dma("gpsimd", guv[buf][:, j], srcs[j].rearrange("(c p) o -> p c o", p=128),
                        writes=[t_gu[buf]], dbg=f"s{s}g{gi} ldgu{j}")
                    dma("sync", dsts[j].rearrange("(c p) o -> p c o", p=128), guv[buf][:, j],
                        reads=[t_gu[buf]], writes=[tb(f"ewbf{j}")],
                        dbg=f"s{s}g{gi} stgu{j}")
            else:
                srcs = [swbf[j] for j in range(2)] if e_id < 0 else [ewbf[e_id, j] for j in range(2)]
                for j in range(2):
                    dma(q, guv[buf][:, j], srcs[j].rearrange("(c p) o -> p c o", p=128),
                        reads=[tb(f"ewbf{j}")], writes=[t_gu[buf]],
                        dbg=f"s{s}g{gi} ldgub{j}")

        def emit_wd_load(gi):
            q = "gpsimd"
            e_id = (gi % 9) - 1
            if first_pass_s and gi <= 8:
                src = sw[2] if e_id < 0 else ew[e_id, 2]
                dst = swbf[2] if e_id < 0 else ewbf[e_id, 2]
                dma("gpsimd", wdv, src.rearrange("(c p) o -> p c o", p=128),
                    writes=[t_wd], dbg=f"s{s}g{gi} ldwd")
                dma("sync", dst.rearrange("(c p) o -> p c o", p=128), wdv,
                    reads=[t_wd], writes=[tb("ewbf2")], dbg=f"s{s}g{gi} stwd")
            else:
                src = swbf[2] if e_id < 0 else ewbf[e_id, 2]
                dma(q, wdv, src.rearrange("(c p) o -> p c o", p=128),
                    reads=[tb("ewbf2")], writes=[t_wd], dbg=f"s{s}g{gi} ldwdb")

        def emit_hu(u, ln2=None):
            hf, w_ = u // 18, u % 18
            gi_hf, t2 = w_ // 2, w_ % 2
            gi = u // 2
            e_id = gi_hf - 1
            buf = gi % 2
            tl = 2 * hf + t2
            ts_ = slice(hf * 1024 + t2 * QT, hf * 1024 + (t2 + 1) * QT)
            up = u % 2
            if e_id >= 0:
                S.op("tensor", lambda e, e_id=e_id, ts_=ts_: e.matmul(
                    PB[5][:], basis_sb[:, e_id * 128:(e_id + 1) * 128],
                    wT_sb[0:8, ts_], start=True, stop=True),
                    reads=[tb("basis"), t_wT], writes=[t_pb[5]], dbg=f"u{u} wB mm")
                S.op("scalar", lambda e, up=up: e.copy(wBsb[up], PB[5][:]),
                     reads=[t_pb[5]], writes=[t_wB[up]], dbg=f"u{u} wB ev")
            for fc in range(8):
                hb = fc % 3
                ub = 3 + fc % 2
                sli = fc % 2

                def mm_hu(e, buf=buf, fc=fc, ts_=ts_, hb=hb, ub=ub):
                    for cc in range(8):
                        e.matmul(PB[hb][:], guv[buf][:, 0, cc, fc * 128:(fc + 1) * 128],
                                 x1nbv[:, cc, ts_], start=(cc == 0), stop=(cc == 7))
                    last = None
                    for cc in range(8):
                        last = e.matmul(PB[ub][:], guv[buf][:, 1, cc, fc * 128:(fc + 1) * 128],
                                        x1nbv[:, cc, ts_], start=(cc == 0), stop=(cc == 7))
                    return last
                S.op("tensor", mm_hu, reads=[t_gu[buf], t_QBt[tl]],
                     writes=[t_pb[hb], t_pb[ub]], dbg=f"u{u}f{fc} hu")
                if ln2 is not None and fc >= 1:
                    ln2_mmpair(fc - 1)
                if u == 0 and fc >= 4:
                    emit_glogit(8 + fc)
                    if fc == 7:
                        emit_gchain()
                if u == 1:
                    emit_gtr(2 * fc)
                    emit_gtr(2 * fc + 1)
                S.op("scalar", lambda e, hb=hb, sli=sli: e.activation(
                    sl[sli], PB[hb][:], AF.Silu),
                    reads=[t_pb[hb]], writes=[t_sl[sli]], dbg=f"u{u}f{fc} silu")
                S.op("vector", lambda e, ub=ub, sli=sli, fc=fc, up=up: e.tensor_tensor(
                    mscv[up][:, fc, :], sl[sli], PB[ub][:], ALU.mult),
                    reads=[t_sl[sli], t_pb[ub]], writes=[t_msc[up]],
                    dbg=f"u{u}f{fc} msc")
                if ln2 is not None:
                    ln2_cast(ln2[1], fc)
            if ln2 is not None:
                ln2_mmpair(7)
                ln2_extract(ln2[1])

        def emit_down(u, ln2=None):
            hf, w_ = u // 18, u % 18
            gi_hf, t2 = w_ // 2, w_ % 2
            e_id = gi_hf - 1
            tl = 2 * hf + t2
            ts_ = slice(hf * 1024 + t2 * QT, hf * 1024 + (t2 + 1) * QT)
            tloc = slice(t2 * QT, (t2 + 1) * QT)
            up = u % 2
            for dc in range(8):
                db = 6 if dc % 2 == 0 else 7

                def mm_d(e, dc=dc, db=db, up=up):
                    last = None
                    for fc in range(8):
                        last = e.matmul(PB[db][:], wdv[:, fc, dc * 128:(dc + 1) * 128],
                                        mscv[up][:, fc, :], start=(fc == 0), stop=(fc == 7))
                    return last
                S.op("tensor", mm_d, reads=[t_wd, t_msc[up]], writes=[t_pb[db]],
                     dbg=f"u{u}d{dc} down")
                if ln2 is not None and dc >= 1:
                    ln2_mmpair(dc - 1)
                sti = dc % 2
                if e_id == -1:
                    dma("sync", dstg[sti], spill[s][dc * 128:(dc + 1) * 128, ts_],
                        reads=[t_spill4[dc % 4]], writes=[t_dstg[sti]],
                        dbg=f"u{u}d{dc} x1nf")
                    S.op("vector", lambda e, db=db, dc=dc, tloc=tloc, sti=sti:
                         e.tensor_tensor(accv[:, dc, tloc], PB[db][:], dstg[sti],
                                         ALU.add),
                         reads=[t_pb[db], t_dstg[sti]], writes=[t_acc],
                         dbg=f"u{u}d{dc} acc init")
                else:
                    S.op("vector", lambda e, db=db, sti=sti, up=up: e.tensor_tensor(
                        dstg[sti], PB[db][:], wBsb[up], ALU.mult),
                        reads=[t_pb[db], t_wB[up]], writes=[t_dstg[sti]],
                        dbg=f"u{u}d{dc} dw")
                    S.op("vector", lambda e, dc=dc, tloc=tloc, sti=sti: e.tensor_tensor(
                        accv[:, dc, tloc], accv[:, dc, tloc], dstg[sti], ALU.add),
                        reads=[t_acc, t_dstg[sti]], writes=[t_acc],
                        dbg=f"u{u}d{dc} acc add")
                if ln2 is not None:
                    ln2_cast(ln2[1], dc)
            if ln2 is not None:
                ln2_mmpair(7)
                ln2_extract(ln2[1])
        def ln2_cast(t2, dc):
            stloc = slice(t2 * QT, (t2 + 1) * QT)
            S.op("vector", lambda e, dc=dc, stloc=stloc: e.tensor_copy(
                x1pb, accv[:, dc, stloc]),
                reads=[t_acc], writes=[t_x1pb], dbg="ln2 cast")
            S.op("vector", lambda e, dc=dc, stloc=stloc: e.tensor_tensor(
                sqb, accv[:, dc, stloc], accv[:, dc, stloc], ALU.mult),
                reads=[t_acc], writes=[t_sqb], dbg="ln2 sq")

        def ln2_mmpair(dc):
            S.op("tensor", lambda e, dc=dc: e.matmul(
                PB[5][0:1, :], ones_col_b[:], x1pb, start=(dc == 0), stop=(dc == 7)),
                reads=[tb("ones_col_b"), t_x1pb], writes=[t_pb[5]], dbg="ln2 sum")
            S.op("tensor", lambda e, dc=dc: e.matmul(
                PB[5][32:33, :], ones_col_b[:], sqb, start=(dc == 0), stop=(dc == 7)),
                reads=[tb("ones_col_b"), t_sqb], writes=[t_pb[5]], dbg="ln2 sqmm")

        def ln2_extract(st=0):
            sx = "b" if st else ""
            S.op("scalar", lambda e, st=st: e.activation(
                rset[st][0], PB[5][0:1, :], AF.Identity, scale=1.0 / C),
                reads=[t_pb[5]], writes=[tb("r0" + sx)], dbg="ln2 mean")
            S.op("scalar", lambda e, st=st: e.activation(
                rset[st][1], PB[5][32:33, :], AF.Identity, scale=1.0 / C),
                reads=[t_pb[5]], writes=[tb("r1" + sx)], dbg="ln2 ex2")

        def emit_ln2solo(hf, t2):
            for dc in range(8):
                ln2_cast(t2, dc)
                ln2_mmpair(dc)
            ln2_extract()

        def emit_ln2fin(hf, t2):
                st = t2
                sx = "b" if st else ""
                r0, r1, r2 = rset[st]
                onesR = onesF[st * 32:st * 32 + 1, :]
                ts_glob = slice(hf * 1024 + t2 * QT, hf * 1024 + (t2 + 1) * QT)
                tloc = slice(t2 * QT, (t2 + 1) * QT)
                S.op("vector", lambda e, r0=r0, r2=r2: e.tensor_tensor(r2, r0, r0, ALU.mult),
                     reads=[tb("r0" + sx)], writes=[tb("r2" + sx)], dbg="ln2 m2")
                S.op("vector", lambda e, r1=r1, r2=r2: e.tensor_tensor(r1, r1, r2, ALU.subtract),
                     reads=[tb("r1" + sx), tb("r2" + sx)], writes=[tb("r1" + sx)], dbg="ln2 var")
                S.op("vector", lambda e, r1=r1: e.tensor_scalar(r1, r1, 1e-5, None, ALU.add),
                     reads=[tb("r1" + sx)], writes=[tb("r1" + sx)], dbg="ln2 eps")
                S.op("scalar", lambda e, r1=r1: e.activation(r1, r1, AF.Ln),
                     reads=[tb("r1" + sx)], writes=[tb("r1" + sx)], dbg="ln2 ln")
                S.op("scalar", lambda e, r1=r1, r2=r2: e.activation(r2, r1, AF.Exp, scale=-0.5),
                     reads=[tb("r1" + sx)], writes=[tb("r2" + sx)], dbg="ln2 rstd")
                S.op("vector", lambda e, r0=r0, r2=r2: e.tensor_tensor(r0, r0, r2, ALU.mult),
                     reads=[tb("r0" + sx), tb("r2" + sx)], writes=[tb("r0" + sx)], dbg="ln2 b0")
                S.op("vector", lambda e, r0=r0: e.tensor_scalar(r0, r0, -1.0, None, ALU.mult),
                     reads=[tb("r0" + sx)], writes=[tb("r0" + sx)], dbg="ln2 b")
                S.op("tensor", lambda e, r2=r2, onesR=onesR: e.matmul(PB[5][:], onesR, r2,
                                                  start=True, stop=True),
                     reads=[tb("onesF"), tb("r2" + sx)], writes=[t_pb[5]], dbg="ln2 bca")
                S.op("scalar", lambda e: e.copy(aB, PB[5][:]),
                     reads=[t_pb[5]], writes=[t_aB], dbg="ln2 bca ev")
                S.op("tensor", lambda e, r0=r0, onesR=onesR: e.matmul(PB[5][:], onesR, r0,
                                                  start=True, stop=True),
                     reads=[tb("onesF"), tb("r0" + sx)], writes=[t_pb[5]], dbg="ln2 bcb")
                S.op("scalar", lambda e: e.copy(bB, PB[5][:]),
                     reads=[t_pb[5]], writes=[t_bB], dbg="ln2 bcb ev")
                for dc in range(8):
                    sti = dc % 2
                    S.op("vector", lambda e, dc=dc, tloc=tloc: e.tensor_tensor(
                        accv[:, dc, tloc], accv[:, dc, tloc], aB, ALU.mult),
                        reads=[t_acc, t_aB], writes=[t_acc], dbg="ln2 ap1")
                    S.op("vector", lambda e, dc=dc, tloc=tloc, sti=sti: e.tensor_tensor(
                        dstg[sti], accv[:, dc, tloc], bB, ALU.add),
                        reads=[t_acc, t_bB], writes=[t_dstg[sti]], dbg="ln2 ap2")
                    dma("sync", outT[s][dc * 128:(dc + 1) * 128, ts_glob], dstg[sti],
                        reads=[t_dstg[sti]], writes=[tb(f"outT{dc % 4}")],
                        dbg=f"s{s}h{hf}t{t2}d{dc} out")

        emit_gu_load(0)
        emit_wd_load(0)
        emit_gu_load(1)
        for u in range(36):
            if u == 19:
                emit_ln2fin(0, 0)
            if u == 20:
                emit_ln2fin(0, 1)
            emit_hu(u, ln2=(0, 0) if u == 18 else ((0, 1) if u == 19 else None))
            if u % 2 == 1 and u // 2 + 2 <= 17:
                emit_gu_load(u // 2 + 2)
            if u >= 1:
                emit_down(u - 1, ln2=(1, 0) if u - 1 == 34 else None)
                if u % 2 == 0 and u >= 2 and u // 2 <= 17:
                    emit_wd_load(u // 2)
        emit_down(35, ln2=(1, 1))
        emit_ln2fin(1, 0)
        emit_ln2fin(1, 1)

    fence_reads = [tb(f"outT{i}") for i in range(4)]
    if dev_spill_out:
        fence_reads += [tb("wdbg"), tb("ydbg"), tb("xpdbg")]
        fence_reads += [tb(f"spill_{nseq - 1}_{i}") for i in range(4)]
    S.op("gpsimd", lambda e: e.nop(), reads=fence_reads, dbg="fence gp")
    S.op("vector", lambda e: e.engine_nop(), reads=fence_reads, dbg="fence v")

    return nc, S


def _rope_perm():
    perm = np.zeros(C, dtype=np.int64)
    for h in range(NH):
        for i in range(32):
            perm[h * HD + i] = h * HD + 2 * i
            perm[h * HD + 32 + i] = h * HD + 2 * i + 1
    return perm


def _rope_tables():
    import jax
    import jax.numpy as jnp
    with jax.default_device(jax.devices("cpu")[0]):
        t = jnp.arange(T, dtype=jnp.float32)[:, None]
        i = jnp.arange(HD // 2, dtype=jnp.float32)
        theta = jnp.power(10000.0, 2.0 * i / HD)
        ang = t * theta[None, :]
        cos = np.asarray(jnp.cos(ang), np.float32)
        sin = np.asarray(jnp.sin(ang), np.float32)
    crep = np.zeros((128, T), np.float32)
    srep = np.zeros((128, T), np.float32)
    for hh in range(2):
        for j in range(64):
            r = hh * 64 + j
            if j < 32:
                crep[r] = cos[:, j]
                srep[r] = -sin[:, j]
            else:
                crep[r] = cos[:, j - 32]
                srep[r] = sin[:, j - 32]
    return crep, srep


def _prep_core_inputs(xc, Wq, bq, Wk, bk, Wv, bv, Wp, bp, gate_w, sw_g, sw_u, sw_d,
                      ew_g, ew_u, ew_d):
    perm = _rope_perm()
    wqkvp = np.stack([np.ascontiguousarray(Wq[perm].T), np.ascontiguousarray(Wk[perm].T),
                      np.ascontiguousarray(Wv.T), np.ascontiguousarray(Wp.T)])
    b_arr = np.stack([bq[perm], bk[perm], bv, bp]).reshape(4, 8, 128)
    b_arr = np.ascontiguousarray(b_arr.transpose(2, 0, 1).reshape(128, 32))
    bvB = np.ascontiguousarray(np.broadcast_to(bv[None, :], (128, C)))
    sw = np.ascontiguousarray(np.stack([sw_g, sw_u, sw_d]))
    ewst = np.ascontiguousarray(
        np.stack([np.stack([ew_g[e], ew_u[e], ew_d[e]]) for e in range(NE)]))
    crep, srep = _rope_tables()
    swm = np.zeros((128, 128), np.float32)
    for m in range(128):
        swm[m ^ 32, m] = 1.0
    trim = (np.arange(128)[None, :] >= np.arange(128)[:, None]).astype(np.float32)
    ident = np.eye(128, dtype=np.float32)
    basis = np.zeros((NE, NE * 128), np.float32)
    for e in range(NE):
        basis[e, e * 128:(e + 1) * 128] = 1.0
    xTc = np.ascontiguousarray(xc.transpose(0, 2, 1))
    return {
        "xT": xTc, "wqkvp": wqkvp, "b_arr": b_arr, "bvB_in": bvB,
        "gate_w_in": np.ascontiguousarray(gate_w), "sw": sw, "ew": ewst,
        "crep_in": crep, "srep_in": srep, "sswap_in": swm, "tri_in": trim,
        "ident_in": ident, "basis_in": basis,
    }


_CACHE = {}


def _get_program():
    if "nc" not in _CACHE:
        nc, S = build_program()
        S.finalize(nc)
        _CACHE["nc"] = nc
    return _CACHE["nc"]


def kernel(L0_attn_x, L0_attn_Wq, L0_attn_bq, L0_attn_Wk, L0_attn_bk, L0_attn_Wv,
           L0_attn_bv, L0_attn_Wp, L0_attn_bp, L0_ffn_gate_w, L0_ffn_sw_gate,
           L0_ffn_sw_up, L0_ffn_sw_down, L0_ffn_ew_gate, L0_ffn_ew_up, L0_ffn_ew_down):
    x = np.asarray(L0_attn_x)
    nc = _get_program()
    args = [np.asarray(a) for a in (
        L0_attn_Wq, L0_attn_bq, L0_attn_Wk, L0_attn_bk, L0_attn_Wv, L0_attn_bv,
        L0_attn_Wp, L0_attn_bp, L0_ffn_gate_w, L0_ffn_sw_gate, L0_ffn_sw_up,
        L0_ffn_sw_down, L0_ffn_ew_gate, L0_ffn_ew_up, L0_ffn_ew_down)]
    in_maps = [_prep_core_inputs(x[c * NSEQ:(c + 1) * NSEQ], *args) for c in range(8)]
    res = run_bass_kernel_spmd(nc, in_maps, core_ids=list(range(8)))
    out = np.empty((32, T, C), np.float32)
    for c in range(8):
        out[c * NSEQ:(c + 1) * NSEQ] = res.results[c]["outT"].transpose(0, 2, 1)
    return out


# version 30
# speedup vs baseline: 1.1666x; 1.1666x over previous
import numpy as np
import concourse.bass as bass
import concourse.mybir as mybir
from concourse.bass_utils import run_bass_kernel_spmd

F32 = mybir.dt.float32
BF16 = mybir.dt.bfloat16
AF = mybir.ActivationFunctionType
ALU = mybir.AluOpType
AX = mybir.AxisListType

NSEQ = 4
T = 2048
C = 1024
NH = 16
HD = 64
NE = 8
QT = 512
NKC = T // 128
NQT = T // QT
NSAFE = 4


class _Instr:
    __slots__ = ("engine", "fn", "flagged", "pos", "gseq", "sem_val", "sem_key", "dbg")

    def __init__(self, engine, fn, pos, gseq, dbg):
        self.engine = engine
        self.fn = fn
        self.flagged = False
        self.pos = pos
        self.gseq = gseq
        self.sem_val = None
        self.sem_key = None
        self.dbg = dbg


class TBuf:
    __slots__ = ("name", "rng", "last_write", "readers", "overlaps")

    def __init__(self, name, rng=None):
        self.name = name
        self.rng = rng
        self.last_write = None
        self.readers = []
        self.overlaps = []


class Sched:
    ENGINES = ("tensor", "vector", "scalar", "sync", "gpsimd")
    DMA_ENGINES = ("sync", "gpsimd")

    def __init__(self):
        self.q = {e: [] for e in self.ENGINES}
        self.deps = {e: [] for e in self.ENGINES}
        self.bufs = {}
        self.arena_bufs = []

    def buf(self, name, rng=None):
        b = self.bufs.get(name)
        if b is None:
            b = TBuf(name, rng)
            self.bufs[name] = b
            if rng is not None:
                for o in self.arena_bufs:
                    if o.rng[0] < rng[1] and rng[0] < o.rng[1]:
                        o.overlaps.append(b)
                        b.overlaps.append(o)
                self.arena_bufs.append(b)
        return b

    def op(self, engine, fn, reads=(), writes=(), dbg=""):
        self.n_emitted = getattr(self, "n_emitted", 0) + 1
        ins = _Instr(engine, fn, len(self.q[engine]), self.n_emitted, dbg)
        deps = []
        for b in reads:
            if b.last_write is not None:
                deps.append(b.last_write)
        for b in writes:
            grp = [b] + b.overlaps
            for m in grp:
                if m.last_write is not None:
                    deps.append(m.last_write)
                deps.extend(m.readers)
        self.q[engine].append(ins)
        self.deps[engine].append(deps)
        for b in reads:
            b.readers.append(ins)
        for b in writes:
            for m in [b] + b.overlaps:
                m.last_write = ins
                m.readers = []
        return ins

    def finalize(self, nc):
        for e in self.ENGINES:
            for i, deps in enumerate(self.deps[e]):
                for d in deps:
                    if d.engine == e == "tensor":
                        continue
                    if d.engine != e or i - d.pos < NSAFE:
                        d.flagged = True
        counters = {}
        allins = [ins for e in self.ENGINES for ins in self.q[e] if ins.flagged]
        allins.sort(key=lambda i: i.gseq)
        for ins in allins:
            if ins.sem_key is not None:
                k = ins.sem_key
                counters[k] = counters.get(k, 0) + 16
            else:
                k = f"eng_{ins.engine}"
                counters[k] = counters.get(k, 0) + 1
                ins.sem_key = k
            ins.sem_val = counters[k]
        sems = {k: nc.semaphore(f"s_{k}").__enter__() for k in counters}
        for e in self.ENGINES:
            sems.setdefault(f"eng_{e}", None)
        self.sems = sems
        sched = self

        with nc.Block() as block:
            def mk_body(e):
                def body(eng):
                    waited = {}
                    for i, ins in enumerate(sched.q[e]):
                        need = {}
                        for d in sched.deps[e][i]:
                            if d.engine == e == "tensor":
                                continue
                            if d.engine == e and i - d.pos >= NSAFE:
                                continue
                            assert d.sem_val is not None, f"dep unflagged: {d.dbg}"
                            if d.sem_val > need.get(d.sem_key, 0):
                                need[d.sem_key] = d.sem_val
                        for k in sorted(need):
                            if waited.get(k, 0) < need[k]:
                                eng.wait_ge(sems[k], need[k])
                                waited[k] = need[k]
                        ret = ins.fn(eng)
                        if ins.flagged:
                            assert ret is not None, f"no carrier: {e}:{i} {ins.dbg}"
                            inc = 16 if ins.sem_key in counters and not ins.sem_key.startswith("eng_") else 1
                            ret.then_inc(sems[ins.sem_key], inc)
                return body

            for e in self.ENGINES:
                if self.q[e]:
                    getattr(block, e)(mk_body(e))
        return sems


ARENA_BYTES = 190464

O_QB = 0
O_KB = 32768
O_VSB = 65536
O_WDB0 = 98816
O_WDB1 = 107008
O_XB = 115200
O_QEB = 147968
O_C1 = 150016
O_C2 = 152064
O_PT = 115200
O_YSB = 118272
O_X1P = 126464
O_RESST = 142848
O_GU0 = 32768
O_GU1 = 65536
O_WD = 98304
O_MSC0 = 114688
O_MSC1 = 122880
O_SL = 131072
O_DSTG = 135168
O_ACC = 139264
O_CREP = 172032
O_SREP = 176128
O_WB = 180224
O_AB = 182272
O_BB = 184320
O_RECB = 186368
O_X1PB = 188416
O_SQB = 189440
assert O_SQB + 1024 <= ARENA_BYTES


def build_program(nseq=NSEQ, dev_spill_out=False):
    nc = bass.Bass("TRN2", target_bir_lowering=False)
    S = Sched()

    xT = nc.dram_tensor("xT", [nseq, C, T], F32, kind="ExternalInput").ap()
    wqkvp = nc.dram_tensor("wqkvp", [4, C, C], F32, kind="ExternalInput").ap()
    b_arr = nc.dram_tensor("b_arr", [128, 32], F32, kind="ExternalInput").ap()
    bvB_in = nc.dram_tensor("bvB_in", [128, C], F32, kind="ExternalInput").ap()
    gate_w_in = nc.dram_tensor("gate_w_in", [C, NE], F32, kind="ExternalInput").ap()
    sw = nc.dram_tensor("sw", [3, C, C], F32, kind="ExternalInput").ap()
    ew = nc.dram_tensor("ew", [NE, 3, C, C], F32, kind="ExternalInput").ap()
    crep_in = nc.dram_tensor("crep_in", [128, T], F32, kind="ExternalInput").ap()
    srep_in = nc.dram_tensor("srep_in", [128, T], F32, kind="ExternalInput").ap()
    sswap_in = nc.dram_tensor("sswap_in", [128, 128], F32, kind="ExternalInput").ap()
    tri_in = nc.dram_tensor("tri_in", [128, 128], F32, kind="ExternalInput").ap()
    ident_in = nc.dram_tensor("ident_in", [128, 128], F32, kind="ExternalInput").ap()
    basis_in = nc.dram_tensor("basis_in", [NE, NE * 128], F32, kind="ExternalInput").ap()
    outT = nc.dram_tensor("outT", [nseq, C, T], F32, kind="ExternalOutput").ap()
    spill_kind = "ExternalOutput" if dev_spill_out else "Internal"
    spill = nc.dram_tensor("spill", [nseq, C, T], F32, kind=spill_kind).ap()
    ewbf = nc.dram_tensor("ewbf", [NE, 3, C, C], BF16, kind="Internal").ap()
    swbf = nc.dram_tensor("swbf", [3, C, C], BF16, kind="Internal").ap()
    if dev_spill_out:
        wdbg = nc.dram_tensor("wdbg", [nseq, NE, T], BF16, kind="ExternalOutput").ap()
        ydbg = nc.dram_tensor("ydbg", [128, 8 * QT], BF16, kind="ExternalOutput").ap()
        xpdbg = nc.dram_tensor("xpdbg", [128, 8 * QT], F32, kind="ExternalOutput").ap()

    arena = nc.alloc_sbuf_tensor("arena", [128, ARENA_BYTES], mybir.dt.uint8)

    def av(off, nbytes, dt):
        return arena[:, off:off + nbytes].bitcast(dt)

    b_sb = nc.alloc_sbuf_tensor("b_sb", [128, 32], F32)
    bvB = nc.alloc_sbuf_tensor("bvB", [128, C], BF16)
    sswap = nc.alloc_sbuf_tensor("sswap", [128, 128], BF16)
    tri = nc.alloc_sbuf_tensor("tri", [128, 128], BF16)
    identf = nc.alloc_sbuf_tensor("identf", [128, 128], F32)
    ones_col_b = nc.alloc_sbuf_tensor("ones_col_b", [128, 1], BF16)
    onesF = nc.alloc_sbuf_tensor("onesF", [128, 128], F32)
    selB = nc.alloc_sbuf_tensor("selB", [128, 128], F32)
    rec2b = nc.alloc_sbuf_tensor("rec2b", [128, QT], F32)
    gw_sb = nc.alloc_sbuf_tensor("gw_sb", [128, 8 * NE], BF16)
    basis_sb = nc.alloc_sbuf_tensor("basis_sb", [NE, NE * 128], BF16)
    wT_sb = nc.alloc_sbuf_tensor("wT_sb", [NE, T], BF16)
    rows4 = nc.alloc_sbuf_tensor("rows4", [128, 3 * QT], F32)
    gsc = nc.alloc_sbuf_tensor("gsc", [128, 448], F32)

    r0 = rows4[0:1, 0:QT]
    r1 = rows4[0:1, QT:2 * QT]
    r2 = rows4[0:1, 2 * QT:3 * QT]
    rset = [(r0, r1, r2),
            (rows4[32:33, 0:QT], rows4[32:33, QT:2 * QT], rows4[32:33, 2 * QT:3 * QT])]
    rec_rows = [[rec2b[0:1, :], rec2b[32:33, :]],
                [rec2b[64:65, :], rec2b[96:97, :]]]
    rec2v = [rec2b[0:33, :], rec2b[64:97, :]]
    QB = av(O_QB, 32768, BF16).rearrange("p (k t) -> p k t", t=T)
    KB = av(O_KB, 32768, BF16).rearrange("p (k t) -> p k t", t=T)
    V4 = av(O_VSB, 33280, BF16).rearrange("p (a b c) -> p a b c", b=16, c=65)
    Wdb = [av(O_WDB0, 8192, BF16).rearrange("p (k t) -> p k t", t=512),
           av(O_WDB1, 8192, BF16).rearrange("p (k t) -> p k t", t=512)]
    xb = av(O_XB, 32768, BF16).rearrange("p (k t) -> p k t", t=T)
    qeb = [av(O_QEB + i * 1024, 1024, BF16) for i in range(2)]
    c1 = av(O_C1, 2048, F32)
    c2 = av(O_C2, 2048, F32)
    Pt = [av(O_PT + i * 1024, 1024, BF16) for i in range(3)] + [av(O_QEB, 1024, BF16)]
    y_v = av(O_YSB, 8192, BF16).rearrange("p (k t) -> p k t", t=QT)
    x1pv = av(O_X1P, 16384, F32).rearrange("p (k t) -> p k t", t=QT)
    resst = [av(O_RESST + i * 2048, 2048, F32) for i in range(2)]
    aB = av(O_AB, 2048, F32)
    bB = av(O_BB, 2048, F32)
    recB = av(O_RECB, 2048, F32)
    x1pb = av(O_X1PB, 1024, BF16)
    sqb = av(O_SQB, 1024, BF16)
    crepB = av(O_CREP, 4096, BF16)
    srepB = av(O_SREP, 4096, BF16)
    x1nbv = QB
    guv = [av(O_GU0 + j * 32768, 32768, BF16).rearrange("p (m k t) -> p m k t",
                                                        m=2, t=1024)
           for j in range(2)]
    wdv = av(O_WD, 16384, BF16).rearrange("p (k t) -> p k t", t=1024)
    mscv = [av(O_MSC0 + j * 8192, 8192, BF16).rearrange("p (k t) -> p k t", t=QT)
            for j in range(2)]
    accv = av(O_ACC, 32768, F32).rearrange("p (k t) -> p k t", t=1024)
    sl = [av(O_SL + i * 2048, 2048, F32) for i in range(2)]
    dstg = [av(O_DSTG + i * 2048, 2048, F32) for i in range(2)]
    wBsb = [av(O_WB + i * 1024, 1024, BF16) for i in range(2)]
    gA, gB, gC = gsc[:, 0:128], gsc[:, 128:256], gsc[:, 256:384]
    gA3 = gA.rearrange("p (g e) -> p g e", e=8)
    gB3 = gB.rearrange("p (g e) -> p g e", e=8)
    gC3 = gC.rearrange("p (g e) -> p g e", e=8)
    gm1, gm2, grd = gsc[:, 384:400], gsc[:, 400:416], gsc[:, 416:432]
    gm13 = gm1.rearrange("p (g o) -> p g o", o=1)
    gm23 = gm2.rearrange("p (g o) -> p g o", o=1)
    grd3 = grd.rearrange("p (g o) -> p g o", o=1)

    PB = [nc.alloc_psum_tensor(f"pb{i}", [128, 512], F32) for i in range(8)]

    def tb(name):
        return S.buf(name)

    def ab_(name, off, nbytes):
        return S.buf(name, (off, off + nbytes))

    t_QBt = [tb(f"QBt{j}") for j in range(4)]
    t_KB = ab_("KB", O_KB, 32768)
    t_V = ab_("Vsb", O_VSB, 33280)
    t_Wdb = [ab_("Wdb0", O_WDB0, 8192), ab_("Wdb1", O_WDB1, 8192)]
    t_xbc = [tb(f"xbc{j}") for j in range(4)]
    t_xb_rng = ab_("xbrng", O_XB, 32768)
    t_qeb = [ab_(f"qeb{i}", O_QEB + i * 1024, 1024) for i in range(2)]
    t_c1 = ab_("c1", O_C1, 2048)
    t_c2 = ab_("c2", O_C2, 2048)
    t_Pt = [ab_(f"Pt{i}", O_PT + i * 1024, 1024) for i in range(3)] + [ab_("Pt3", O_QEB, 1024)]
    t_ysb = ab_("ysb", O_YSB, 8192)
    t_x1p = ab_("x1p", O_X1P, 16384)
    t_resst = [ab_(f"resst{i}", O_RESST + i * 2048, 2048) for i in range(2)]
    t_gu = [ab_(f"gu{j}", O_GU0 + j * 32768, 32768) for j in range(2)]
    t_wd = ab_("wd", O_WD, 16384)
    t_msc = [ab_(f"msc{j}", O_MSC0 + j * 8192, 8192) for j in range(2)]
    t_acc = ab_("acc", O_ACC, 32768)
    t_sl = [ab_(f"sl{i}", O_SL + i * 2048, 2048) for i in range(2)]
    t_dstg = [ab_(f"dstg{i}", O_DSTG + i * 2048, 2048) for i in range(2)]
    t_crep = ab_("crepB", O_CREP, 4096)
    t_srep = ab_("srepB", O_SREP, 4096)
    t_x1pb = ab_("x1pb", O_X1PB, 1024)
    t_sqb = ab_("sqb", O_SQB, 1024)
    t_aB = ab_("aB", O_AB, 2048)
    t_bB = ab_("bB", O_BB, 2048)
    t_recB = ab_("recB", O_RECB, 2048)
    t_wB = [ab_(f"wB{i}", O_WB + i * 1024, 1024) for i in range(2)]
    t_gsc = tb("gsc")
    t_pb = [tb(f"pb{i}") for i in range(8)]
    t_pb5s = [tb(f"pb5s{k}") for k in range(4)]
    for k in range(4):
        t_pb5s[k].overlaps.append(t_pb[5])
        t_pb[5].overlaps.append(t_pb5s[k])

    def dma(engine, out_ap, in_ap, reads=(), writes=(), dbg="", skey=None):
        ins = S.op(engine, lambda e, o=out_ap, i=in_ap: e.dma_start(out=o, in_=i),
                   reads=reads, writes=writes, dbg=dbg)
        ins.flagged = True
        assert writes, f"dma without written buffer: {dbg}"
        ins.sem_key = skey if skey is not None else ("d_" + writes[0].name)
        return ins

    dma("sync", b_sb[:], b_arr, writes=[tb("b_sb")], dbg="ld b_sb")
    dma("gpsimd", bvB[:], bvB_in, writes=[tb("bvB")], dbg="ld bvB")
    dma("gpsimd", tri[:], tri_in, writes=[tb("tri")], dbg="ld tri")
    dma("sync", identf[:], ident_in, writes=[tb("identf")], dbg="ld identf")
    dma("gpsimd", sswap[:], sswap_in, writes=[tb("sswap")], dbg="ld sswap")
    dma("gpsimd", basis_sb[:], basis_in, writes=[tb("basis")], dbg="ld basis")
    dma("gpsimd", gw_sb[:].rearrange("p (c e) -> p c e", c=8),
        gate_w_in.rearrange("(c p) e -> p c e", p=128), writes=[tb("gw_sb")], dbg="ld gw")
    S.op("vector", lambda e: e.memset(ones_col_b[:], 1.0), writes=[tb("ones_col_b")],
         dbg="ones col")
    S.op("vector", lambda e: e.memset(onesF[:], 1.0), writes=[tb("onesF")],
         dbg="ones f")
    S.op("vector", lambda e: e.memset(selB[:], 0.0), writes=[tb("selB")], dbg="selB 0")
    S.op("vector", lambda e: e.memset(selB[0:1, 0:64], 1.0), writes=[tb("selB")], dbg="selB a")
    S.op("vector", lambda e: e.memset(selB[32:33, 64:128], 1.0), writes=[tb("selB")], dbg="selB b")
    S.op("vector", lambda e: e.memset(selB[64:65, 0:64], 1.0), writes=[tb("selB")], dbg="selB c")
    S.op("vector", lambda e: e.memset(selB[96:97, 64:128], 1.0), writes=[tb("selB")], dbg="selB d")
    S.op("vector", lambda e: e.memset(rec2b[:], 1.0),
         writes=[tb("rec2"), tb("rec2_0"), tb("rec2_1")], dbg="rec2 1")
    warm_op = av(0, 2048, BF16)

    def mm_warm(e):
        last = None
        for _ in range(16):
            last = e.matmul(PB[0][:, 0:512], warm_op[:, 0:128], warm_op[:, 0:512],
                            start=True, stop=True)
        return last
    S.op("tensor", mm_warm, dbg="warmup")
    dma("gpsimd", crepB, crep_in, writes=[t_crep], dbg="ld crep")
    dma("gpsimd", srepB, srep_in, writes=[t_srep], dbg="ld srep")

    for s in range(nseq):
        t_spill4 = [tb(f"spill_{s}_{i}") for i in range(4)]
        S.op("vector", lambda e: e.memset(V4[:, :, :, 64:65], 1.0), writes=[t_V],
             dbg=f"s{s} V ones")
        xsrc = xT[s].rearrange("(c p) t -> p c t", p=128)
        for j in range(4):
            dma("gpsimd", xb[:, :, j * QT:(j + 1) * QT], xsrc[:, :, j * QT:(j + 1) * QT],
                writes=[t_xbc[j], t_xb_rng], dbg=f"s{s} ld xb{j}")

        def emit_w_load(wi):
            w, h = wi // 2, wi % 2
            bufi = wi % 2
            dma("gpsimd", Wdb[bufi],
                wqkvp[w].rearrange("(c p) o -> p c o", p=128)[:, :, h * 512:(h + 1) * 512],
                writes=[t_Wdb[bufi]], dbg=f"s{s} ldW{w}h{h}")

        emit_w_load(0)
        emit_w_load(1)
        qk_i = 0
        for wi in range(8):
            w, h = wi // 2, wi % 2
            bufi = wi % 2
            wv = Wdb[bufi]
            if w in (0, 1):
                dstv = QB if w == 0 else KB
                for qt in range(NQT):
                    ts_ = slice(qt * QT, (qt + 1) * QT)
                    t_dst = t_QBt[qt] if w == 0 else t_KB
                    for oc in range(h * 4, h * 4 + 4):
                        bk = (0, 1, 3, 4)[qk_i % 4]
                        sbk2 = (2, 5)[qk_i % 2]
                        qbi = qk_i % 2
                        qk_i += 1
                        lo = (oc - h * 4) * 128

                        def mm_qkv(e, wv=wv, lo=lo, ts_=ts_, bk=bk):
                            last = None
                            for cc in range(8):
                                last = e.matmul(PB[bk][:], wv[:, cc, lo:lo + 128],
                                                xb[:, cc, ts_], start=(cc == 0), stop=(cc == 7))
                            return last
                        S.op("tensor", mm_qkv,
                             reads=[t_Wdb[bufi], t_xbc[qt], t_xb_rng], writes=[t_pb[bk]],
                             dbg=f"s{s}w{w}oc{oc}q{qt} mm")
                        bia = b_sb[:, w * 8 + oc:w * 8 + oc + 1]
                        S.op("scalar", lambda e, bk=bk, bia=bia, qbi=qbi: e.activation(
                            qeb[qbi][:], PB[bk][:], AF.Identity, bias=bia),
                            reads=[t_pb[bk], tb("b_sb")], writes=[t_qeb[qbi]],
                            dbg=f"s{s}w{w}oc{oc}q{qt} qeb")
                        S.op("tensor", lambda e, qbi=qbi, sbk2=sbk2: e.matmul(
                            PB[sbk2][:], sswap[:], qeb[qbi][:], start=True, stop=True),
                            reads=[tb("sswap"), t_qeb[qbi]], writes=[t_pb[sbk2]],
                            dbg=f"s{s}w{w}oc{oc}q{qt} swap")
                        S.op("vector", lambda e, ts_=ts_, sbk2=sbk2: e.tensor_tensor(
                            c2[:], PB[sbk2][:], srepB[:, ts_], ALU.mult),
                            reads=[t_pb[sbk2], t_srep], writes=[t_c2], dbg="rope c2")
                        S.op("vector", lambda e, ts_=ts_, qbi=qbi: e.tensor_tensor(
                            c1[:], qeb[qbi][:], crepB[:, ts_], ALU.mult),
                            reads=[t_qeb[qbi], t_crep], writes=[t_c1], dbg="rope c1")
                        S.op("vector", lambda e, dstv=dstv, oc=oc, ts_=ts_: e.tensor_tensor(
                            dstv[:, oc, ts_], c1[:], c2[:], ALU.add),
                            reads=[t_c1, t_c2], writes=[t_dst], dbg="rope sum")
            elif w == 2:
                for tt in range(NKC):
                    bk = (0, 1, 3, 4)[tt % 4]

                    def mm_v(e, wv=wv, tt=tt, bk=bk):
                        last = None
                        for cc in range(8):
                            last = e.matmul(PB[bk][:], xb[:, cc, tt * 128:(tt + 1) * 128],
                                            wv[:, cc, :], start=(cc == 0), stop=(cc == 7))
                        return last
                    S.op("tensor", mm_v, reads=[t_Wdb[bufi], t_xbc[tt // 4], t_xb_rng],
                         writes=[t_pb[bk]], dbg=f"s{s}V t{tt}h{h}")
                    pin = PB[bk][:].rearrange("p (h d) -> p h d", h=8)[:, :, 0:64]
                    vout = V4[:, tt, h * 8:(h + 1) * 8, 0:64]
                    bin_ = bvB[:].rearrange("p (h d) -> p h d", h=16)[:, h * 8:(h + 1) * 8, :]
                    S.op("vector", lambda e, vout=vout, pin=pin, bin_=bin_: e.tensor_tensor(
                        vout, pin, bin_, ALU.add),
                        reads=[t_pb[bk], tb("bvB")], writes=[t_V], dbg=f"s{s}V ev t{tt}h{h}")
            if wi + 2 < 8:
                emit_w_load(wi + 2)

        pending_ln = [None]
        for qt in range(NQT):
            ts_ = slice(qt * QT, (qt + 1) * QT)

            def emit_score_pair(hp, kc, qt=qt):
                oc = hp
                j = kc - 4 * qt
                colsoff = max(0, j * 128)
                for par in range(2):
                    rb = par * 64
                    sbk = (kc % 2) * 2 + par
                    S.op("tensor", lambda e, sbk=sbk, rb=rb, oc=oc, kc=kc, colsoff=colsoff, qt=qt:
                         e.matmul(PB[sbk][:, colsoff:QT],
                                  KB[rb:rb + 64, oc, kc * 128:(kc + 1) * 128],
                                  QB[rb:rb + 64, oc, qt * QT + colsoff:(qt + 1) * QT],
                                  start=True, stop=True, tile_position=(rb, 0)),
                         reads=[t_KB, t_QBt[qt]], writes=[t_pb[sbk]],
                         dbg=f"s{s}q{qt}p{hp}.{par}k{kc} sc")

            def emit_py_pair(hp, kc, qt=qt, ts_=ts_):
                kcN = 4 * qt + 4
                j = kc - 4 * qt
                colsoff = max(0, j * 128)
                N = QT - colsoff
                for par in range(2):
                    hh = 2 * hp + par
                    sbk = (kc % 2) * 2 + par
                    pbi = (kc * 2 + par) % 4
                    S.op("scalar", lambda e, sbk=sbk, pbi=pbi, colsoff=colsoff, N=N:
                         e.activation(Pt[pbi][:, 0:N], PB[sbk][:, colsoff:QT],
                                      AF.Exp, scale=0.125),
                         reads=[t_pb[sbk]], writes=[t_Pt[pbi]],
                         dbg=f"s{s}q{qt}h{hh}k{kc} exp")
                    if j >= 0:
                        S.op("vector", lambda e, pbi=pbi: e.tensor_tensor(
                            Pt[pbi][:, 0:128], Pt[pbi][:, 0:128], tri[:], ALU.mult),
                            reads=[t_Pt[pbi], tb("tri")], writes=[t_Pt[pbi]],
                            dbg=f"s{s}q{qt}h{hh}k{kc} tri")
                for par in range(2):
                    hh = 2 * hp + par
                    pbi = (kc * 2 + par) % 4
                    ybk = 4 + (hp % 2) * 2 + par
                    S.op("tensor", lambda e, ybk=ybk, kc=kc, hh=hh, pbi=pbi, colsoff=colsoff,
                         N=N, kcN=kcN: e.matmul(
                             PB[ybk][0:65, colsoff:QT], V4[:, kc, hh, :], Pt[pbi][:, 0:N],
                             start=(kc == 0), stop=(kc == kcN - 1)),
                         reads=[t_V, t_Pt[pbi]], writes=[t_pb[ybk]],
                         dbg=f"s{s}q{qt}h{hh}k{kc} y")

            def emit_flash(hp, qt=qt):
                kcN = 4 * qt + 4
                emit_score_pair(hp, 0)
                for kc in range(kcN):
                    if kc + 1 < kcN:
                        emit_score_pair(hp, kc + 1)
                    emit_py_pair(hp, kc)
                pp = hp % 2
                for par in range(2):
                    ybk = 4 + pp * 2 + par
                    S.op("vector", lambda e, ybk=ybk, pp=pp, par=par: e.reciprocal(
                        rec_rows[pp][par], PB[ybk][64:65, :]),
                        reads=[t_pb[ybk]], writes=[tb(f"rec2_{pp}")],
                        dbg=f"s{s}q{qt}p{hp}.{par} rec")

            def emit_evac(hp, qt=qt):
                oc = hp
                pp = hp % 2
                bc = 0 if pp == 0 else 2
                S.op("tensor", lambda e, pp=pp, bc=bc: e.matmul(
                    PB[bc][:], selB[pp * 64:pp * 64 + 33, :], rec2v[pp],
                    start=True, stop=True),
                    reads=[tb("selB"), tb(f"rec2_{pp}")], writes=[t_pb[bc]],
                    dbg=f"s{s}q{qt}p{hp} bc")
                S.op("scalar", lambda e, bc=bc: e.copy(recB[0:64, :], PB[bc][0:64, :]),
                     reads=[t_pb[bc]], writes=[t_recB], dbg=f"s{s}q{qt}p{hp} bcev")
                S.op("scalar", lambda e, bc=bc: e.copy(c1[0:64, :], PB[bc][64:128, :]),
                     reads=[t_pb[bc]], writes=[t_c1], dbg=f"s{s}q{qt}p{hp} bcev2")
                for par in range(2):
                    ybk = 4 + pp * 2 + par
                    rsrc = recB if par == 0 else c1
                    t_rs = t_recB if par == 0 else t_c1
                    S.op("vector", lambda e, ybk=ybk, par=par, oc=oc, rsrc=rsrc: e.tensor_tensor(
                        y_v[par * 64:par * 64 + 64, oc, :], PB[ybk][0:64, :],
                        rsrc[0:64, :], ALU.mult),
                        reads=[t_pb[ybk], t_rs], writes=[t_ysb],
                        dbg=f"s{s}q{qt}p{hp}.{par} yev")

            def emit_resld(oc, qt=qt, ts_=ts_):
                dma("gpsimd", resst[oc % 2], xT[s][oc * 128:(oc + 1) * 128, ts_],
                    writes=[t_resst[oc % 2]], dbg=f"s{s}q{qt}oc{oc} resld")

            for hp in range(8):
                emit_flash(hp)
                if hp == 1 and pending_ln[0] is not None:
                    pending_ln[0]()
                    pending_ln[0] = None
                if hp == 5:
                    emit_resld(0)
                    emit_resld(1)
                if hp > 0:
                    emit_evac(hp - 1)
            emit_evac(7)
            if dev_spill_out and s == 0 and qt == 0:
                dma("sync", ydbg, av(O_YSB, 8192, BF16), reads=[t_ysb],
                    writes=[tb("ydbg")], dbg="ydbg")
            for oc in range(8):
                sti = oc % 2
                pbk = 5 if oc % 2 == 0 else 2
                half, lo = (0, oc * 128) if oc < 4 else (1, (oc - 4) * 128)

                def mm_proj(e, pbk=pbk, half=half, lo=lo):
                    last = None
                    for cc in range(8):
                        last = e.matmul(PB[pbk][:], Wdb[half][:, cc, lo:lo + 128],
                                        y_v[:, cc, :], start=(cc == 0), stop=(cc == 7))
                    return last
                S.op("tensor", mm_proj, reads=[t_Wdb[half], t_ysb], writes=[t_pb[pbk]],
                     dbg=f"s{s}q{qt}oc{oc} proj")
                bia = b_sb[:, 24 + oc:24 + oc + 1]
                S.op("vector", lambda e, pbk=pbk, oc=oc, bia=bia, sti=sti:
                     e.scalar_tensor_tensor(x1pv[:, oc, :], PB[pbk][:], bia,
                                            resst[sti], ALU.add, ALU.add),
                     reads=[t_pb[pbk], tb("b_sb"), t_resst[sti]], writes=[t_x1p],
                     dbg=f"s{s}q{qt}oc{oc} x1p")
                if oc + 2 < 8:
                    emit_resld(oc + 2)
                S.op("vector", lambda e, oc=oc: e.tensor_copy(x1pb, x1pv[:, oc, :]),
                     reads=[t_x1p], writes=[t_x1pb], dbg="st cast")
                S.op("vector", lambda e, oc=oc: e.tensor_tensor(
                    sqb, x1pv[:, oc, :], x1pv[:, oc, :], ALU.mult),
                    reads=[t_x1p], writes=[t_sqb], dbg="st sq")
                S.op("tensor", lambda e, oc=oc: e.matmul(
                    PB[6][0:1, :], ones_col_b[:], x1pb, start=(oc == 0), stop=(oc == 7)),
                    reads=[tb("ones_col_b"), t_x1pb], writes=[t_pb[6]], dbg="st sum")
                S.op("tensor", lambda e, oc=oc: e.matmul(
                    PB[7][0:1, :], ones_col_b[:], sqb, start=(oc == 0), stop=(oc == 7)),
                    reads=[tb("ones_col_b"), t_sqb], writes=[t_pb[7]], dbg="st sq mm")

            S.op("vector", lambda e: e.tensor_scalar(r0, PB[6][0:1, :], 1.0 / C,
                                                     None, ALU.mult),
                 reads=[t_pb[6]], writes=[tb("r0")], dbg="ln mean")
            S.op("vector", lambda e: e.tensor_scalar(r1, PB[7][0:1, :], 1.0 / C,
                                                     None, ALU.mult),
                 reads=[t_pb[7]], writes=[tb("r1")], dbg="ln ex2")

            def make_ln1(qt=qt, ts_=ts_):
                def emit_ln1():
                    S.op("vector", lambda e: e.tensor_tensor(r2, r0, r0, ALU.mult),
                         reads=[tb("r0")], writes=[tb("r2")], dbg="ln m2")
                    S.op("vector", lambda e: e.tensor_tensor(r1, r1, r2, ALU.subtract),
                         reads=[tb("r1"), tb("r2")], writes=[tb("r1")], dbg="ln var")
                    S.op("vector", lambda e: e.tensor_scalar(r1, r1, 1e-5, None, ALU.add),
                         reads=[tb("r1")], writes=[tb("r1")], dbg="ln eps")
                    S.op("scalar", lambda e: e.activation(r1, r1, AF.Ln),
                         reads=[tb("r1")], writes=[tb("r1")], dbg="ln ln")
                    S.op("scalar", lambda e: e.activation(r2, r1, AF.Exp, scale=-0.5),
                         reads=[tb("r1")], writes=[tb("r2")], dbg="ln rstd")
                    S.op("vector", lambda e: e.tensor_tensor(r0, r0, r2, ALU.mult),
                         reads=[tb("r0"), tb("r2")], writes=[tb("r0")], dbg="ln b0")
                    S.op("vector", lambda e: e.tensor_scalar(r0, r0, -1.0, None, ALU.mult),
                         reads=[tb("r0")], writes=[tb("r0")], dbg="ln b")
                    S.op("tensor", lambda e: e.matmul(PB[1][:], onesF[0:1, :], r2,
                                                      start=True, stop=True),
                         reads=[tb("onesF"), tb("r2")], writes=[t_pb[1]], dbg="ln bca")
                    S.op("scalar", lambda e: e.copy(aB, PB[1][:]),
                         reads=[t_pb[1]], writes=[t_aB], dbg="ln bca ev")
                    S.op("tensor", lambda e: e.matmul(PB[1][:], onesF[0:1, :], r0,
                                                      start=True, stop=True),
                         reads=[tb("onesF"), tb("r0")], writes=[t_pb[1]], dbg="ln bcb")
                    S.op("scalar", lambda e: e.copy(bB, PB[1][:]),
                         reads=[t_pb[1]], writes=[t_bB], dbg="ln bcb ev")


                    for oc in range(8):
                        sti = oc % 2
                        S.op("vector", lambda e, oc=oc: e.tensor_tensor(
                            x1pv[:, oc, :], x1pv[:, oc, :], aB, ALU.mult),
                            reads=[t_x1p, t_aB], writes=[t_x1p], dbg="ln ap1")
                        S.op("vector", lambda e, oc=oc, sti=sti: e.tensor_tensor(
                            resst[sti], x1pv[:, oc, :], bB, ALU.add),
                            reads=[t_x1p, t_bB], writes=[t_resst[sti]], dbg="ln ap2")
                        S.op("vector", lambda e, oc=oc, ts_=ts_, sti=sti: e.tensor_copy(
                            x1nbv[:, oc, ts_], resst[sti]),
                            reads=[t_resst[sti]], writes=[t_QBt[qt]], dbg="ln x1nb")
                        dma("sync", spill[s][oc * 128:(oc + 1) * 128, ts_], resst[sti],
                            reads=[t_resst[sti]], writes=[t_spill4[oc % 4]],
                            dbg=f"s{s}q{qt}oc{oc} spill", skey=f"d_sp{oc % 4}")

                return emit_ln1

            if dev_spill_out and s == 0 and qt == 0:
                dma("sync", xpdbg, av(O_X1P, 16384, F32), reads=[t_x1p],
                    writes=[tb("xpdbg")], dbg="xpdbg")
            pending_ln[0] = make_ln1()
        pending_ln[0]()
        pending_ln[0] = None

        gv = gw_sb[:].rearrange("p (c e) -> p c e", c=8)
        t_wT = tb(f"wT_{s}")

        def emit_glogit(pi):
            def mm_gl(e, pi=pi):
                last = None
                for cc in range(8):
                    last = e.matmul(PB[5][:, 8 * pi:8 * pi + 8],
                                    x1nbv[:, cc, pi * 128:(pi + 1) * 128],
                                    gv[:, cc, :], start=(cc == 0), stop=(cc == 7))
                return last
            S.op("tensor", mm_gl, reads=[tb("gw_sb"), t_QBt[pi // 4]],
                 writes=[t_pb[5]], dbg=f"s{s}g{pi} mm")

        def emit_gchain():
            S.op("scalar", lambda e: e.copy(gA, PB[5][:, 0:128]),
                 reads=[t_pb[5]], writes=[t_gsc], dbg=f"s{s} g evac")
            S.op("vector", lambda e: e.reduce_max(gm13, gA3, AX.X),
                 reads=[t_gsc], writes=[t_gsc], dbg="g m1")
            S.op("vector", lambda e: e.tensor_tensor(
                gB3, gA3, gm13.broadcast_to([128, 16, 8]), ALU.subtract),
                reads=[t_gsc], writes=[t_gsc], dbg="g sub")
            S.op("scalar", lambda e: e.activation(gA, gB, AF.Exp),
                 reads=[t_gsc], writes=[t_gsc], dbg="g exp")
            S.op("vector", lambda e: e.reduce_max(gm13, gA3, AX.X),
                 reads=[t_gsc], writes=[t_gsc], dbg="g m1b")
            S.op("vector", lambda e: e.tensor_tensor(
                gB3, gA3, gm13.broadcast_to([128, 16, 8]), ALU.is_ge),
                reads=[t_gsc], writes=[t_gsc], dbg="g geq")
            S.op("vector", lambda e: e.scalar_tensor_tensor(
                gC, gB, -1e30, gA, ALU.mult, ALU.add),
                reads=[t_gsc], writes=[t_gsc], dbg="g gg2")
            S.op("vector", lambda e: e.reduce_max(gm23, gC3, AX.X),
                 reads=[t_gsc], writes=[t_gsc], dbg="g m2")
            S.op("vector", lambda e: e.tensor_tensor(
                gB3, gA3, gm23.broadcast_to([128, 16, 8]), ALU.is_ge),
                reads=[t_gsc], writes=[t_gsc], dbg="g sel")
            S.op("vector", lambda e: e.tensor_tensor(gC, gA, gB, ALU.mult),
                 reads=[t_gsc], writes=[t_gsc], dbg="g num")
            S.op("vector", lambda e: e.tensor_tensor(gm1, gm1, gm2, ALU.add),
                 reads=[t_gsc], writes=[t_gsc], dbg="g den")
            S.op("vector", lambda e: e.reciprocal(grd, gm1),
                 reads=[t_gsc], writes=[t_gsc], dbg="g rd")
            S.op("vector", lambda e: e.tensor_tensor(
                gC3, gC3, grd3.broadcast_to([128, 16, 8]), ALU.mult),
                reads=[t_gsc], writes=[t_gsc], dbg="g w")

        for pi in range(12):
            emit_glogit(pi)

        def emit_gtr(pi):
            cs = (pi % 4) * 128
            S.op("tensor", lambda e, pi=pi, cs=cs: e.transpose(
                PB[5][0:8, cs:cs + 128], gC3[:, pi, :], identf[:]),
                reads=[t_gsc, tb("identf")], writes=[t_pb5s[pi % 4]],
                dbg=f"s{s} gtr{pi}")
            S.op("scalar", lambda e, pi=pi, cs=cs: e.copy(
                wT_sb[0:8, pi * 128:(pi + 1) * 128], PB[5][0:8, cs:cs + 128]),
                reads=[t_pb5s[pi % 4]], writes=[t_wT], dbg=f"s{s} gtrc{pi}")

        first_pass_s = (s == 0)

        def emit_gu_load(gi):
            buf = gi % 2
            q = "gpsimd"
            e_id = (gi % 9) - 1
            if first_pass_s and gi <= 8:
                srcs = [sw[j] for j in range(2)] if e_id < 0 else [ew[e_id, j] for j in range(2)]
                dsts = [swbf[j] for j in range(2)] if e_id < 0 else [ewbf[e_id, j] for j in range(2)]
                for j in range(2):
                    dma("gpsimd", guv[buf][:, j], srcs[j].rearrange("(c p) o -> p c o", p=128),
                        writes=[t_gu[buf]], dbg=f"s{s}g{gi} ldgu{j}")
                    dma("sync", dsts[j].rearrange("(c p) o -> p c o", p=128), guv[buf][:, j],
                        reads=[t_gu[buf]], writes=[tb(f"ewbf{j}")],
                        dbg=f"s{s}g{gi} stgu{j}")
            else:
                srcs = [swbf[j] for j in range(2)] if e_id < 0 else [ewbf[e_id, j] for j in range(2)]
                for j in range(2):
                    dma(q, guv[buf][:, j], srcs[j].rearrange("(c p) o -> p c o", p=128),
                        reads=[tb(f"ewbf{j}")], writes=[t_gu[buf]],
                        dbg=f"s{s}g{gi} ldgub{j}")

        def emit_wd_load(gi):
            q = "gpsimd"
            e_id = (gi % 9) - 1
            if first_pass_s and gi <= 8:
                src = sw[2] if e_id < 0 else ew[e_id, 2]
                dst = swbf[2] if e_id < 0 else ewbf[e_id, 2]
                dma("gpsimd", wdv, src.rearrange("(c p) o -> p c o", p=128),
                    writes=[t_wd], dbg=f"s{s}g{gi} ldwd")
                dma("sync", dst.rearrange("(c p) o -> p c o", p=128), wdv,
                    reads=[t_wd], writes=[tb("ewbf2")], dbg=f"s{s}g{gi} stwd")
            else:
                src = swbf[2] if e_id < 0 else ewbf[e_id, 2]
                dma(q, wdv, src.rearrange("(c p) o -> p c o", p=128),
                    reads=[tb("ewbf2")], writes=[t_wd], dbg=f"s{s}g{gi} ldwdb")

        def emit_hu(u, ln2=None):
            hf, w_ = u // 18, u % 18
            gi_hf, t2 = w_ // 2, w_ % 2
            gi = u // 2
            e_id = gi_hf - 1
            buf = gi % 2
            tl = 2 * hf + t2
            ts_ = slice(hf * 1024 + t2 * QT, hf * 1024 + (t2 + 1) * QT)
            up = u % 2
            if e_id >= 0:
                S.op("tensor", lambda e, e_id=e_id, ts_=ts_: e.matmul(
                    PB[5][:], basis_sb[:, e_id * 128:(e_id + 1) * 128],
                    wT_sb[0:8, ts_], start=True, stop=True),
                    reads=[tb("basis"), t_wT], writes=[t_pb[5]], dbg=f"u{u} wB mm")
                S.op("scalar", lambda e, up=up: e.copy(wBsb[up], PB[5][:]),
                     reads=[t_pb[5]], writes=[t_wB[up]], dbg=f"u{u} wB ev")
            for fc in range(8):
                hb = fc % 3
                ub = 3 + fc % 2
                sli = fc % 2

                def mm_hu(e, buf=buf, fc=fc, ts_=ts_, hb=hb, ub=ub):
                    for cc in range(8):
                        e.matmul(PB[hb][:], guv[buf][:, 0, cc, fc * 128:(fc + 1) * 128],
                                 x1nbv[:, cc, ts_], start=(cc == 0), stop=(cc == 7))
                    last = None
                    for cc in range(8):
                        last = e.matmul(PB[ub][:], guv[buf][:, 1, cc, fc * 128:(fc + 1) * 128],
                                        x1nbv[:, cc, ts_], start=(cc == 0), stop=(cc == 7))
                    return last
                S.op("tensor", mm_hu, reads=[t_gu[buf], t_QBt[tl]],
                     writes=[t_pb[hb], t_pb[ub]], dbg=f"u{u}f{fc} hu")
                if ln2 is not None and fc >= 1:
                    ln2_mmpair(fc - 1)
                if u == 0 and fc >= 4:
                    emit_glogit(8 + fc)
                    if fc == 7:
                        emit_gchain()
                if u == 1:
                    emit_gtr(2 * fc)
                    emit_gtr(2 * fc + 1)
                S.op("scalar", lambda e, hb=hb, sli=sli: e.activation(
                    sl[sli], PB[hb][:], AF.Silu),
                    reads=[t_pb[hb]], writes=[t_sl[sli]], dbg=f"u{u}f{fc} silu")
                S.op("vector", lambda e, ub=ub, sli=sli, fc=fc, up=up: e.tensor_tensor(
                    mscv[up][:, fc, :], sl[sli], PB[ub][:], ALU.mult),
                    reads=[t_sl[sli], t_pb[ub]], writes=[t_msc[up]],
                    dbg=f"u{u}f{fc} msc")
                if ln2 is not None:
                    ln2_cast(ln2[1], fc)
            if ln2 is not None:
                ln2_mmpair(7)
                ln2_extract(ln2[1])

        def emit_down(u, ln2=None):
            hf, w_ = u // 18, u % 18
            gi_hf, t2 = w_ // 2, w_ % 2
            e_id = gi_hf - 1
            tl = 2 * hf + t2
            ts_ = slice(hf * 1024 + t2 * QT, hf * 1024 + (t2 + 1) * QT)
            tloc = slice(t2 * QT, (t2 + 1) * QT)
            up = u % 2
            for dc in range(8):
                db = 6 if dc % 2 == 0 else 7

                def mm_d(e, dc=dc, db=db, up=up):
                    last = None
                    for fc in range(8):
                        last = e.matmul(PB[db][:], wdv[:, fc, dc * 128:(dc + 1) * 128],
                                        mscv[up][:, fc, :], start=(fc == 0), stop=(fc == 7))
                    return last
                S.op("tensor", mm_d, reads=[t_wd, t_msc[up]], writes=[t_pb[db]],
                     dbg=f"u{u}d{dc} down")
                if ln2 is not None and dc >= 1:
                    ln2_mmpair(dc - 1)
                sti = dc % 2
                if e_id == -1:
                    dma("sync", dstg[sti], spill[s][dc * 128:(dc + 1) * 128, ts_],
                        reads=[t_spill4[dc % 4]], writes=[t_dstg[sti]],
                        dbg=f"u{u}d{dc} x1nf")
                    S.op("vector", lambda e, db=db, dc=dc, tloc=tloc, sti=sti:
                         e.tensor_tensor(accv[:, dc, tloc], PB[db][:], dstg[sti],
                                         ALU.add),
                         reads=[t_pb[db], t_dstg[sti]], writes=[t_acc],
                         dbg=f"u{u}d{dc} acc init")
                else:
                    S.op("vector", lambda e, db=db, sti=sti, up=up: e.tensor_tensor(
                        dstg[sti], PB[db][:], wBsb[up], ALU.mult),
                        reads=[t_pb[db], t_wB[up]], writes=[t_dstg[sti]],
                        dbg=f"u{u}d{dc} dw")
                    S.op("vector", lambda e, dc=dc, tloc=tloc, sti=sti: e.tensor_tensor(
                        accv[:, dc, tloc], accv[:, dc, tloc], dstg[sti], ALU.add),
                        reads=[t_acc, t_dstg[sti]], writes=[t_acc],
                        dbg=f"u{u}d{dc} acc add")
                if ln2 is not None:
                    ln2_cast(ln2[1], dc)
            if ln2 is not None:
                ln2_mmpair(7)
                ln2_extract(ln2[1])
        def ln2_cast(t2, dc):
            stloc = slice(t2 * QT, (t2 + 1) * QT)
            S.op("vector", lambda e, dc=dc, stloc=stloc: e.tensor_copy(
                x1pb, accv[:, dc, stloc]),
                reads=[t_acc], writes=[t_x1pb], dbg="ln2 cast")
            S.op("vector", lambda e, dc=dc, stloc=stloc: e.tensor_tensor(
                sqb, accv[:, dc, stloc], accv[:, dc, stloc], ALU.mult),
                reads=[t_acc], writes=[t_sqb], dbg="ln2 sq")

        def ln2_mmpair(dc):
            S.op("tensor", lambda e, dc=dc: e.matmul(
                PB[5][0:1, :], ones_col_b[:], x1pb, start=(dc == 0), stop=(dc == 7)),
                reads=[tb("ones_col_b"), t_x1pb], writes=[t_pb[5]], dbg="ln2 sum")
            S.op("tensor", lambda e, dc=dc: e.matmul(
                PB[5][32:33, :], ones_col_b[:], sqb, start=(dc == 0), stop=(dc == 7)),
                reads=[tb("ones_col_b"), t_sqb], writes=[t_pb[5]], dbg="ln2 sqmm")

        def ln2_extract(st=0):
            sx = "b" if st else ""
            S.op("scalar", lambda e, st=st: e.activation(
                rset[st][0], PB[5][0:1, :], AF.Identity, scale=1.0 / C),
                reads=[t_pb[5]], writes=[tb("r0" + sx)], dbg="ln2 mean")
            S.op("scalar", lambda e, st=st: e.activation(
                rset[st][1], PB[5][32:33, :], AF.Identity, scale=1.0 / C),
                reads=[t_pb[5]], writes=[tb("r1" + sx)], dbg="ln2 ex2")

        def emit_ln2solo(hf, t2):
            for dc in range(8):
                ln2_cast(t2, dc)
                ln2_mmpair(dc)
            ln2_extract()

        def emit_ln2fin(hf, t2):
                st = t2
                sx = "b" if st else ""
                r0, r1, r2 = rset[st]
                onesR = onesF[st * 32:st * 32 + 1, :]
                ts_glob = slice(hf * 1024 + t2 * QT, hf * 1024 + (t2 + 1) * QT)
                tloc = slice(t2 * QT, (t2 + 1) * QT)
                S.op("vector", lambda e, r0=r0, r2=r2: e.tensor_tensor(r2, r0, r0, ALU.mult),
                     reads=[tb("r0" + sx)], writes=[tb("r2" + sx)], dbg="ln2 m2")
                S.op("vector", lambda e, r1=r1, r2=r2: e.tensor_tensor(r1, r1, r2, ALU.subtract),
                     reads=[tb("r1" + sx), tb("r2" + sx)], writes=[tb("r1" + sx)], dbg="ln2 var")
                S.op("vector", lambda e, r1=r1: e.tensor_scalar(r1, r1, 1e-5, None, ALU.add),
                     reads=[tb("r1" + sx)], writes=[tb("r1" + sx)], dbg="ln2 eps")
                S.op("scalar", lambda e, r1=r1: e.activation(r1, r1, AF.Ln),
                     reads=[tb("r1" + sx)], writes=[tb("r1" + sx)], dbg="ln2 ln")
                S.op("scalar", lambda e, r1=r1, r2=r2: e.activation(r2, r1, AF.Exp, scale=-0.5),
                     reads=[tb("r1" + sx)], writes=[tb("r2" + sx)], dbg="ln2 rstd")
                S.op("vector", lambda e, r0=r0, r2=r2: e.tensor_tensor(r0, r0, r2, ALU.mult),
                     reads=[tb("r0" + sx), tb("r2" + sx)], writes=[tb("r0" + sx)], dbg="ln2 b0")
                S.op("vector", lambda e, r0=r0: e.tensor_scalar(r0, r0, -1.0, None, ALU.mult),
                     reads=[tb("r0" + sx)], writes=[tb("r0" + sx)], dbg="ln2 b")
                S.op("tensor", lambda e, r2=r2, onesR=onesR: e.matmul(PB[5][:], onesR, r2,
                                                  start=True, stop=True),
                     reads=[tb("onesF"), tb("r2" + sx)], writes=[t_pb[5]], dbg="ln2 bca")
                S.op("scalar", lambda e: e.copy(aB, PB[5][:]),
                     reads=[t_pb[5]], writes=[t_aB], dbg="ln2 bca ev")
                S.op("tensor", lambda e, r0=r0, onesR=onesR: e.matmul(PB[5][:], onesR, r0,
                                                  start=True, stop=True),
                     reads=[tb("onesF"), tb("r0" + sx)], writes=[t_pb[5]], dbg="ln2 bcb")
                S.op("scalar", lambda e: e.copy(bB, PB[5][:]),
                     reads=[t_pb[5]], writes=[t_bB], dbg="ln2 bcb ev")
                for dc in range(8):
                    sti = dc % 2
                    S.op("vector", lambda e, dc=dc, tloc=tloc: e.tensor_tensor(
                        accv[:, dc, tloc], accv[:, dc, tloc], aB, ALU.mult),
                        reads=[t_acc, t_aB], writes=[t_acc], dbg="ln2 ap1")
                    S.op("vector", lambda e, dc=dc, tloc=tloc, sti=sti: e.tensor_tensor(
                        dstg[sti], accv[:, dc, tloc], bB, ALU.add),
                        reads=[t_acc, t_bB], writes=[t_dstg[sti]], dbg="ln2 ap2")
                    dma("sync", outT[s][dc * 128:(dc + 1) * 128, ts_glob], dstg[sti],
                        reads=[t_dstg[sti]], writes=[tb(f"outT{dc % 4}")],
                        dbg=f"s{s}h{hf}t{t2}d{dc} out")

        emit_gu_load(0)
        emit_wd_load(0)
        emit_gu_load(1)
        for u in range(36):
            if u == 19:
                emit_ln2fin(0, 0)
            if u == 20:
                emit_ln2fin(0, 1)
            emit_hu(u, ln2=(0, 0) if u == 18 else ((0, 1) if u == 19 else None))
            if u % 2 == 1 and u // 2 + 2 <= 17:
                emit_gu_load(u // 2 + 2)
            if u >= 1:
                emit_down(u - 1, ln2=(1, 0) if u - 1 == 34 else None)
                if u % 2 == 0 and u >= 2 and u // 2 <= 17:
                    emit_wd_load(u // 2)
        emit_down(35, ln2=(1, 1))
        emit_ln2fin(1, 0)
        emit_ln2fin(1, 1)

    fence_reads = [tb(f"outT{i}") for i in range(4)]
    if dev_spill_out:
        fence_reads += [tb("wdbg"), tb("ydbg"), tb("xpdbg")]
        fence_reads += [tb(f"spill_{nseq - 1}_{i}") for i in range(4)]
    S.op("gpsimd", lambda e: e.nop(), reads=fence_reads, dbg="fence gp")
    S.op("vector", lambda e: e.engine_nop(), reads=fence_reads, dbg="fence v")

    return nc, S


def _rope_perm():
    perm = np.zeros(C, dtype=np.int64)
    for h in range(NH):
        for i in range(32):
            perm[h * HD + i] = h * HD + 2 * i
            perm[h * HD + 32 + i] = h * HD + 2 * i + 1
    return perm


def _rope_tables():
    import jax
    import jax.numpy as jnp
    with jax.default_device(jax.devices("cpu")[0]):
        t = jnp.arange(T, dtype=jnp.float32)[:, None]
        i = jnp.arange(HD // 2, dtype=jnp.float32)
        theta = jnp.power(10000.0, 2.0 * i / HD)
        ang = t * theta[None, :]
        cos = np.asarray(jnp.cos(ang), np.float32)
        sin = np.asarray(jnp.sin(ang), np.float32)
    crep = np.zeros((128, T), np.float32)
    srep = np.zeros((128, T), np.float32)
    for hh in range(2):
        for j in range(64):
            r = hh * 64 + j
            if j < 32:
                crep[r] = cos[:, j]
                srep[r] = -sin[:, j]
            else:
                crep[r] = cos[:, j - 32]
                srep[r] = sin[:, j - 32]
    return crep, srep


def _prep_core_inputs(xc, Wq, bq, Wk, bk, Wv, bv, Wp, bp, gate_w, sw_g, sw_u, sw_d,
                      ew_g, ew_u, ew_d):
    perm = _rope_perm()
    wqkvp = np.stack([np.ascontiguousarray(Wq[perm].T), np.ascontiguousarray(Wk[perm].T),
                      np.ascontiguousarray(Wv.T), np.ascontiguousarray(Wp.T)])
    b_arr = np.stack([bq[perm], bk[perm], bv, bp]).reshape(4, 8, 128)
    b_arr = np.ascontiguousarray(b_arr.transpose(2, 0, 1).reshape(128, 32))
    bvB = np.ascontiguousarray(np.broadcast_to(bv[None, :], (128, C)))
    sw = np.ascontiguousarray(np.stack([sw_g, sw_u, sw_d]))
    ewst = np.ascontiguousarray(
        np.stack([np.stack([ew_g[e], ew_u[e], ew_d[e]]) for e in range(NE)]))
    crep, srep = _rope_tables()
    swm = np.zeros((128, 128), np.float32)
    for m in range(128):
        swm[m ^ 32, m] = 1.0
    trim = (np.arange(128)[None, :] >= np.arange(128)[:, None]).astype(np.float32)
    ident = np.eye(128, dtype=np.float32)
    basis = np.zeros((NE, NE * 128), np.float32)
    for e in range(NE):
        basis[e, e * 128:(e + 1) * 128] = 1.0
    xTc = np.ascontiguousarray(xc.transpose(0, 2, 1))
    return {
        "xT": xTc, "wqkvp": wqkvp, "b_arr": b_arr, "bvB_in": bvB,
        "gate_w_in": np.ascontiguousarray(gate_w), "sw": sw, "ew": ewst,
        "crep_in": crep, "srep_in": srep, "sswap_in": swm, "tri_in": trim,
        "ident_in": ident, "basis_in": basis,
    }


_CACHE = {}


def _get_program():
    if "nc" not in _CACHE:
        nc, S = build_program()
        S.finalize(nc)
        _CACHE["nc"] = nc
    return _CACHE["nc"]


def kernel(L0_attn_x, L0_attn_Wq, L0_attn_bq, L0_attn_Wk, L0_attn_bk, L0_attn_Wv,
           L0_attn_bv, L0_attn_Wp, L0_attn_bp, L0_ffn_gate_w, L0_ffn_sw_gate,
           L0_ffn_sw_up, L0_ffn_sw_down, L0_ffn_ew_gate, L0_ffn_ew_up, L0_ffn_ew_down):
    x = np.asarray(L0_attn_x)
    nc = _get_program()
    args = [np.asarray(a) for a in (
        L0_attn_Wq, L0_attn_bq, L0_attn_Wk, L0_attn_bk, L0_attn_Wv, L0_attn_bv,
        L0_attn_Wp, L0_attn_bp, L0_ffn_gate_w, L0_ffn_sw_gate, L0_ffn_sw_up,
        L0_ffn_sw_down, L0_ffn_ew_gate, L0_ffn_ew_up, L0_ffn_ew_down)]
    in_maps = [_prep_core_inputs(x[c * NSEQ:(c + 1) * NSEQ], *args) for c in range(8)]
    res = run_bass_kernel_spmd(nc, in_maps, core_ids=list(range(8)))
    out = np.empty((32, T, C), np.float32)
    for c in range(8):
        out[c * NSEQ:(c + 1) * NSEQ] = res.results[c]["outT"].transpose(0, 2, 1)
    return out
